# version 17
# baseline (speedup 1.0000x reference)
import numpy as np
import ml_dtypes
import concourse.bass as bass
import concourse.mybir as mybir
from concourse.bass_utils import run_bass_kernel_spmd
from contextlib import ExitStack

F32 = mybir.dt.float32
BF16 = mybir.dt.bfloat16
FP16 = mybir.dt.float16
AF = mybir.ActivationFunctionType
ALU = mybir.AluOpType

ENGS = ("sync", "gpsimd", "vector", "scalar", "tensor")


def make_cfg(full=True):
    if full:
        D, T, TO, R, F, N, TC = 2048, 2048, 1024, 256, 8192, 8, 512
        SCP = 128
    else:
        D, T, TO, R, F, N, TC = 256, 256, 128, 128, 512, 8, 64
        SCP = 128
    cfg = dict(D=D, T=T, TO=TO, R=R, F=F, N=N, TC=TC, SCP=SCP)
    cfg["DT"] = D // 128
    cfg["FT"] = F // 128
    cfg["RK"] = R // 128
    cfg["NC4"] = T // TC
    cfg["NC2"] = TO // TC
    cfg["O0"] = T - TO
    cfg["DQW"] = min(256, D)
    cfg["DQ"] = D // cfg["DQW"]
    cfg["KS"] = min(512, F)
    cfg["LN_EPS"] = 1e-5
    cfg["W_EPS"] = 1e-8
    return cfg


class Prog:
    ESEM = {"vector": "dve", "scalar": "act", "gpsimd": "pool"}

    def __init__(self, paranoid=True, self_waits=False):
        self.ops = {e: [] for e in ENGS}
        self.cnt = {}
        self.sems = {}
        self.paranoid = paranoid
        if self_waits is True:
            self_waits = {"vector", "scalar", "gpsimd"}
        self.self_waits = set(self_waits) if self_waits else set()
        self.vforce = False
        self.voff = False
        self.vrelax = set()

    def sem(self, name):
        self.cnt.setdefault(name, 0)
        return name

    def op(self, eng, fn, inc=None, dma=False):
        esem = self.ESEM.get(eng)
        incs = []
        if inc is not None:
            if isinstance(inc, str):
                inc = (inc, 1)
            incs.append(inc)
        if (self.paranoid and not dma and esem is not None
                and eng in ("vector", "scalar")):
            if not any(n == esem for n, _ in incs):
                incs.append((esem, 1))
            if ((eng in self.self_waits and not (eng == "vector" and self.voff))
                    or (eng == "vector" and self.vforce)):
                self.wait(eng, esem, self.now(esem))
        if incs:
            for name, by in incs:
                self.sem(name)
                self.cnt[name] += by
            sems = self.sems
            def wrapped(e, fn=fn, incs=tuple(incs)):
                ins = fn(e)
                for name, by in incs:
                    ins = ins.then_inc(sems[name], by)
            self.ops[eng].append(wrapped)
            return self.cnt[inc[0]] if inc is not None else self.cnt[esem]
        self.ops[eng].append(fn)
        return None

    def wait(self, eng, sem_name, val):
        if val is None or val <= 0:
            return
        sems = self.sems
        self.ops[eng].append(lambda e, s=sem_name, v=val: e.wait_ge(sems[s], v))

    def now(self, name):
        return self.cnt.get(name, 0)

    def flush(self, nc):
        pending = {e: ops for e, ops in self.ops.items() if ops}
        if not pending:
            return
        with nc.Block() as blk:
            for eng_name, ops in pending.items():
                getattr(blk, eng_name)(lambda e, ops=ops: [f(e) for f in ops])
        self.ops = {e: [] for e in ENGS}


class Guard:
    def __init__(self, prog):
        self.P = prog
        self.m = {}

    def consumed(self, key, sem, val=None):
        self.m.setdefault(key, {})[sem] = self.P.now(sem) if val is None else val

    def use(self, key, eng):
        for sem, val in self.m.pop(key, {}).items():
            self.P.wait(eng, sem, val)


class _StopBuild(Exception):
    pass


def build_nc(cfg, stop=None, self_waits=("vector",),
             vrelax=("p1cast", "ln1rows", "ln2rows", "xln2", "ffn1", "ffn2",
                     "p4", "p0", "p3apply")):
    D, T, TO, R, F, N, TC, SCP = (cfg[k] for k in
                                  ("D", "T", "TO", "R", "F", "N", "TC", "SCP"))
    DT, FT, RK, NC4, NC2, O0 = (cfg[k] for k in
                                ("DT", "FT", "RK", "NC4", "NC2", "O0"))
    DQ, DQW, KS = cfg["DQ"], cfg["DQW"], cfg["KS"]
    NW = D * F
    FK = F // 128
    DQT = DQW // 128
    KSK = KS // 128
    NKS = F // KS
    WTOT = D * F // 128
    PCW = min(2048, WTOT)
    NPC1 = WTOT // PCW
    NPC2 = WTOT // PCW

    nc = bass.Bass("TRN2", target_bir_lowering=False)
    d_x = nc.dram_tensor("x", [D, T], F32, kind="ExternalInput").ap()
    d_wxp = nc.dram_tensor("wxp", [D, R + 2 * N], F32, kind="ExternalInput").ap()
    d_wdt = nc.dram_tensor("wdt", [R, D], F32, kind="ExternalInput").ap()
    d_bdt = nc.dram_tensor("bdt", [128, DT], F32, kind="ExternalInput").ap()
    d_alog = nc.dram_tensor("alog", [128, DT * N], F32, kind="ExternalInput").ap()
    d_dpar = nc.dram_tensor("dpar", [128, DT], F32, kind="ExternalInput").ap()
    d_w1 = nc.dram_tensor("w1t", [128, WTOT], F32, kind="ExternalInput").ap()
    d_b1 = nc.dram_tensor("b1", [128, FT], F32, kind="ExternalInput").ap()
    d_w2 = nc.dram_tensor("w2t", [128, WTOT], F32, kind="ExternalInput").ap()
    d_b2 = nc.dram_tensor("b2", [128, DT], F32, kind="ExternalInput").ap()
    d_out = nc.dram_tensor("out", [D, TO], F32, kind="ExternalOutput").ap()
    d_x2 = nc.dram_tensor("x2s", [D, TO], F32, kind="Internal").ap()
    d_xln = nc.dram_tensor("xlns", [D, T], BF16, kind="Internal").ap()
    d_thr = nc.dram_tensor("thrs", [1, 2], F32, kind="Internal").ap()
    d_scp = nc.dram_tensor("scps", [128, 2], F32, kind="Internal").ap()

    P = Prog(self_waits=self_waits)
    P.vrelax = set(vrelax)
    G = Guard(P)

    def voff(name):
        P.voff = name in P.vrelax

    def von():
        P.voff = False

    def ckpt(tag, dump=None):
        if stop == tag:
            if dump is not None:
                P.wait("sync", "dve", P.now("dve"))
                P.op("sync", lambda e: e.dma_start(
                    out=d_out[0:dump[1], 0:dump[2]], in_=dump[0]),
                    inc=("s_out0", 16))
                P.wait("gpsimd", "s_out0", P.now("s_out0"))
            P.wait("gpsimd", "dve", P.now("dve"))
            P.wait("gpsimd", "act", P.now("act"))
            P.wait("gpsimd", "pe", P.now("pe"))
            P.flush(nc)
            raise _StopBuild

    ctx = ExitStack()
    try:
      with ctx:
        def sbt(name, shape, dt=F32):
            return ctx.enter_context(nc.sbuf_tensor(name, shape, dt))

        bdt_c = sbt("bdt_c", [128, DT])
        negA = sbt("negA", [128, DT * N])
        dpar_c = sbt("dpar_c", [128, DT])
        b1_c = sbt("b1_c", [128, FT])
        b2_c = sbt("b2_c", [128, DT])
        ones_cb = sbt("ones_cb", [128, 1], BF16)
        ones_rb = sbt("ones_rb", [1, 128], BF16)
        cvals = sbt("cvals", [128, 2])
        srow_bf = sbt("srow_bf", [1, 2 * T], BF16)
        thr_cols = sbt("thr_cols", [128, 4])

        ps = ctx.enter_context(nc.psum_tensor("ps", [128, 4096], F32))
        def bank(b, rows=128, cols=512, c0=0):
            return ps[0:rows, b * 512 + c0: b * 512 + c0 + cols]

        for s in ("s_x0", "s_x1", "s_x2", "s_x3", "s_sm", "s_smg", "s_smg2",
                  "s_wsc0", "s_wsc1", "s_wsc2", "s_w10", "s_w11",
                  "s_w2s0", "s_w2s1", "s_w2s2", "s_w2s3",
                  "s_xo", "s_x2w", "s_x2r0", "s_x2r1", "s_x2e0", "s_x2e1",
                  "s_x2e2", "s_x2e3",
                  "s_out0", "s_out1", "s_br0", "s_br1", "s_thr", "s_xlw", "s_xl0", "s_xl1",
                  "pe", "act", "dve", "pool"):
            P.sem(s)
            P.sems[s] = ctx.enter_context(nc.semaphore(s))

        for dst, src in ((bdt_c, d_bdt), (dpar_c, d_dpar), (b1_c, d_b1),
                         (b2_c, d_b2), (negA, d_alog)):
            P.op("sync", lambda e, dd=dst, ss=src: e.dma_start(out=dd[:], in_=ss),
                 inc=("s_sm", 16))
        sm_all = P.now("s_sm")
        P.wait("scalar", "s_sm", sm_all)
        P.op("scalar", lambda e: e.activation(out=negA[:], in_=negA[:],
                                              func=AF.Exp), inc="act")
        aexp_done = P.now("act")
        P.wait("vector", "act", aexp_done)
        voff("p0")
        P.op("vector", lambda e: e.tensor_scalar(
            out=negA[:], in0=negA[:], scalar1=-1.0, scalar2=None, op0=ALU.mult))
        P.op("vector", lambda e: e.memset(ones_cb[:], 1.0))
        P.op("vector", lambda e: e.memset(ones_rb[:], 1.0))
        P.op("vector", lambda e: e.memset(cvals[:, 0:1], 1e-5))
        P.op("vector", lambda e: e.memset(cvals[:, 1:2], 1.0), inc="dve")
        setup_done = P.now("dve")
        von()

        actx = ExitStack()
        with actx:
            def asbt(name, shape, dt=F32):
                return actx.enter_context(nc.sbuf_tensor(name, shape, dt))
            wdt_bf = asbt("wdt_bf", [128, RK * D], BF16)
            dtr_bf = asbt("dtr_bf", [128, RK * T], BF16)
            bc_rows = asbt("bc_rows", [1, 2 * T], BF16)
            BC_sb = asbt("BC_sb", [32 + N, T])
            ones_f16 = asbt("ones_f16", [128, 1], FP16)
            ones_cf = asbt("ones_cf", [128, 1])
            P.op("vector", lambda e: e.memset(ones_cf[:], 1.0))
            P.op("vector", lambda e: e.memset(ones_f16[:], 1.0), inc="dve")
            ones_sc_done = P.now("dve")
            scpart1 = asbt("scpart1", [128, NPC1])
            scpart2 = asbt("scpart2", [128, NPC2])
            spcol = asbt("spcol", [128, 2])
            dtr3 = dtr_bf[:].rearrange("p (k t) -> p k t", k=RK)
            wdt3 = wdt_bf[:].rearrange("p (k m) -> p k m", k=RK)
            P.op("gpsimd", lambda e: e.dma_start(
                out=wdt3, in_=d_wdt.rearrange("(k p) m -> p k m", p=128)),
                inc=("s_smg", 16))
            smg_all = P.now("s_smg")

            abctx = ExitStack()
            absbt = lambda name, shape, dt=F32: abctx.enter_context(
                nc.sbuf_tensor(name, shape, dt))
            wxp_bf = absbt("wxp_bf", [128, DT * (R + 2 * N)], BF16)
            srow = absbt("srow", [1, 2 * T + 64])
            xbf = [absbt(f"xbf{i}", [128, T], BF16) for i in range(DT)]
            sqt = absbt("sqt", [128, T], BF16)
            xst = [absbt(f"xst{i}", [128, T]) for i in range(4)]
            rstd_bc = absbt("rstd_bc", [128, T], BF16)
            mr_bc = absbt("mr_bc", [128, T], BF16)
            lnt = sqt
            wxp3 = wxp_bf[:].rearrange("p (k m) -> p k m", k=DT)
            P.op("gpsimd", lambda e: e.dma_start(
                out=wxp3, in_=d_wxp.rearrange("(k p) m -> p k m", p=128)),
                inc=("s_smg2", 16))
            smg2_all = P.now("s_smg2")

            x_ld = {}
            def emit_x_load(i):
                q = "sync" if i % 2 == 0 else "scalar"
                G.use(f"xst{i % 4}", q)
                x_ld[i] = P.op(q, lambda e, i=i: e.dma_start(
                    out=xst[i % 4][:], in_=d_x[i * 128:(i + 1) * 128, :]),
                    inc=(f"s_x{i % 4}", 16), dma=True)
            for i in range(min(4, DT)):
                emit_x_load(i)
            P.wait("tensor", "dve", setup_done)
            voff("p1cast")
            for i in range(DT):
                P.wait("vector", f"s_x{i % 4}", x_ld[i])
                P.op("vector", lambda e, i=i: e.tensor_copy(
                    xbf[i][:], xst[i % 4][:]), inc="dve")
                cast_v = P.now("dve")
                G.consumed(f"xst{i % 4}", "dve")
                if i + 4 < DT:
                    emit_x_load(i + 4)
                G.use("sqt", "scalar")
                P.wait("scalar", "dve", cast_v)
                P.op("scalar", lambda e, i=i: e.activation(
                    out=sqt[:], in_=xbf[i][:], func=AF.Square), inc="act")
                sq_done = P.now("act")
                P.wait("tensor", "dve", cast_v)
                P.wait("tensor", "act", sq_done)
                for c in range(NC4):
                    P.op("tensor", lambda e, i=i, c=c: e.matmul(
                        bank(c, rows=1, cols=TC), ones_cb[:],
                        xbf[i][:, c * TC:(c + 1) * TC],
                        start=(i == 0), stop=(i == DT - 1)))
                for c in range(NC4):
                    P.op("tensor", lambda e, i=i, c=c: e.matmul(
                        bank(NC4 + c, rows=1, cols=TC), ones_cb[:],
                        sqt[:, c * TC:(c + 1) * TC],
                        start=(i == 0), stop=(i == DT - 1)),
                        inc="pe" if c == NC4 - 1 else None)
                G.consumed("sqt", "pe")
            stats1_done = P.now("pe")
            von()
            ckpt("p1")

            voff("ln1rows")
            sx = srow[0:1, 0:T]
            sq_r = srow[0:1, T:2 * T]
            tmp_r = srow_bf[0:1, 0:T]
            P.wait("vector", "pe", stats1_done)
            for c in range(NC4):
                P.op("vector", lambda e, c=c: e.tensor_copy(
                    sx[:, c * TC:(c + 1) * TC], bank(c, rows=1, cols=TC)))
            for c in range(NC4):
                P.op("vector", lambda e, c=c: e.tensor_copy(
                    sq_r[:, c * TC:(c + 1) * TC], bank(NC4 + c, rows=1, cols=TC)),
                    inc=("dve", 1) if c == NC4 - 1 else None)
            stats1_ev = P.now("dve")
            for b in range(8):
                G.consumed(f"bank{b}", "dve", stats1_ev)
            P.op("vector", lambda e: e.tensor_scalar(
                out=sx, in0=sx, scalar1=1.0 / D, scalar2=None, op0=ALU.mult))
            P.op("vector", lambda e: e.tensor_scalar(
                out=sq_r, in0=sq_r, scalar1=1.0 / D, scalar2=None, op0=ALU.mult))
            P.op("vector", lambda e: e.tensor_tensor(
                out=tmp_r, in0=sx, in1=sx, op=ALU.mult))
            P.op("vector", lambda e: e.tensor_tensor(
                out=sq_r, in0=sq_r, in1=tmp_r, op=ALU.subtract), inc="dve")
            var_done = P.now("dve")
            P.wait("scalar", "dve", var_done)
            P.op("scalar", lambda e: e.activation(
                out=sq_r, in_=sq_r, func=AF.Ln, bias=cvals[0:1, 0:1], scale=1.0))
            P.op("scalar", lambda e: e.activation(
                out=sq_r, in_=sq_r, func=AF.Exp, scale=-0.5), inc="act")
            rstd_done = P.now("act")
            P.wait("vector", "act", rstd_done)
            P.op("vector", lambda e: e.tensor_tensor(
                out=sx, in0=sx, in1=sq_r, op=ALU.mult))
            rrow_bf = srow_bf[0:1, 0:T]
            mrow_bf = srow_bf[0:1, T:2 * T]
            P.op("vector", lambda e: e.tensor_copy(rrow_bf, sq_r))
            P.op("vector", lambda e: e.tensor_copy(mrow_bf, sx), inc="dve")
            rows_bf_done = P.now("dve")
            P.wait("tensor", "dve", rows_bf_done)

            def k1_bcast(dst, row_ap, nch, bank_pair=(0, 1, 2, 3)):
                assert nch <= len(bank_pair)
                v = None
                for c in range(nch):
                    b = bank_pair[c % len(bank_pair)]
                    G.use(f"bank{b}", "tensor")
                    v = P.op("tensor", lambda e, c=c, b=b, row_ap=row_ap:
                             e.matmul(bank(b, cols=TC), ones_rb[0:1, :],
                                      row_ap[:, c * TC:(c + 1) * TC],
                                      start=True, stop=True), inc="pe")
                P.wait("scalar", "pe", v)
                for c in range(nch):
                    b = bank_pair[c % len(bank_pair)]
                    P.op("scalar", lambda e, c=c, b=b, dst=dst: e.activation(
                        out=dst[:, c * TC:(c + 1) * TC], in_=bank(b, cols=TC),
                        func=AF.Copy), inc="act")
                    G.consumed(f"bank{b}", "act")

            k1_bcast(rstd_bc, rrow_bf, NC4)
            k1_bcast(mr_bc, mrow_bf, NC4)
            bc1_done = P.now("act")
            von()
            ckpt("p2")

            P.wait("vector", "act", bc1_done)
            voff("p3apply")
            xap_v = {}
            for i in range(DT):
                P.op("vector", lambda e, i=i: e.tensor_tensor(
                    out=lnt[:], in0=xbf[i][:], in1=rstd_bc[:], op=ALU.mult))
                P.op("vector", lambda e, i=i: e.tensor_tensor(
                    out=xbf[i][:], in0=lnt[:], in1=mr_bc[:], op=ALU.subtract),
                    inc="dve")
                xl_v = P.now("dve")
                xap_v[i] = xl_v
                P.wait("sync", "dve", xl_v)
                P.op("sync", lambda e, i=i: e.dma_start(
                    out=d_xln[i * 128:(i + 1) * 128, :], in_=xbf[i][:]),
                    inc=("s_xlw", 16))
            xln_done = P.now("dve")
            von()
            xlw_all = P.now("s_xlw")

            P.wait("tensor", "s_sm", sm_all)
            P.wait("tensor", "s_smg", smg_all)
            P.wait("tensor", "s_smg2", smg2_all)
            mchunks = [("dtr", mi, mi * 128, 128) for mi in range(RK)] + \
                      [("B", 0, R, N), ("C", 1, R + N, N)]
            first_mchunk = [True]
            for (kind, mi, m0, msz) in mchunks:
                for c in range(NC4):
                    b = c % 2
                    G.use(f"bank{b}", "tensor")
                    v = None
                    for k in range(DT):
                        if first_mchunk[0]:
                            P.wait("tensor", "dve", xap_v[k])
                        v = P.op("tensor", lambda e, k=k, c=c, m0=m0, msz=msz, b=b:
                                 e.matmul(bank(b, rows=msz, cols=TC),
                                          wxp3[:, k, m0:m0 + msz],
                                          xbf[k][:, c * TC:(c + 1) * TC],
                                          start=(k == 0), stop=(k == DT - 1)),
                                 inc="pe" if k == DT - 1 else None)
                    if kind == "dtr":
                        P.wait("scalar", "pe", v)
                        P.op("scalar", lambda e, mi=mi, c=c, b=b: e.activation(
                            out=dtr3[:, mi, c * TC:(c + 1) * TC],
                            in_=bank(b, cols=TC), func=AF.Copy), inc="act")
                        G.consumed(f"bank{b}", "act")
                    else:
                        row0 = 0 if kind == "B" else 32
                        P.wait("scalar", "pe", v)
                        P.op("scalar", lambda e, row0=row0, c=c, b=b:
                             e.activation(
                                 out=BC_sb[row0:row0 + N, c * TC:(c + 1) * TC],
                                 in_=bank(b, rows=N, cols=TC), func=AF.Copy),
                             inc="act")
                        G.consumed(f"bank{b}", "act")
                first_mchunk[0] = False
            bc_ex_done = P.now("act")

            bcn_done = P.now("act")
            ckpt("p3")
            P.flush(nc)
            abctx.close()

            cctx = ExitStack()
            csbt = lambda name, shape, dt=F32: cctx.enter_context(
                nc.sbuf_tensor(name, shape, dt))
            B_bc = [csbt(f"Bbc{n}", [128, T], BF16) for n in range(N)]
            C_bc = [csbt(f"Cbc{n}", [128, TO], BF16) for n in range(N)]
            xln_t = [csbt(f"xlnt{i}", [128, T], BF16) for i in range(2)]
            sq2_t = csbt("sq2_t", [128, TO], BF16)
            srow2 = csbt("srow2", [1, 2 * TO + 2 * 512 + 8])
            delta_t = [csbt(f"delta_t{i}", [128, T], BF16) for i in range(2)]
            u_t = [csbt(f"u_t{i}", [128, T], BF16) for i in range(2)]
            dA_b = [csbt(f"dAb{i}", [128, T]) for i in range(2)]
            dbu_t = csbt("dbu_t", [128, T], BF16)
            h_t = csbt("h_t", [128, T], BF16)
            y_t = csbt("y_t", [128, TO], BF16)
            xo_t = csbt("xo_t", [128, TO])
            x2_t = csbt("x2_t", [128, TO])
            x2bf_t = csbt("x2bf_t", [128, TO], BF16)
            sp_t = [csbt(f"sp_t{i}", [128, TC]) for i in range(2)]
            wsc_b = [csbt(f"wscb{i}", [128, PCW]) for i in range(3)]
            ty_t = dbu_t[:, 0:TO]
            P.wait("scalar", "s_xlw", xlw_all)
            P.wait("vector", "s_xlw", xlw_all)

            P.wait("gpsimd", "act", bc_ex_done)
            for j in range(2 * N):
                pr = j % 2
                srcrow = j if j < N else 32 + (j - N)
                G.use(f"bcr{pr}", "gpsimd")
                v = P.op("gpsimd", lambda e, pr=pr, srcrow=srcrow: e.dma_start(
                    out=bc_rows[0:1, pr * T:(pr + 1) * T],
                    in_=BC_sb[srcrow:srcrow + 1, :]), inc=(f"s_br{pr}", 16))
                P.wait("tensor", f"s_br{pr}", v)
                if j < N:
                    k1_bcast(B_bc[j], bc_rows[0:1, pr * T:(pr + 1) * T], NC4)
                else:
                    k1_bcast(C_bc[j - N],
                             bc_rows[0:1, pr * T + O0:pr * T + O0 + TO], NC2)
                G.consumed(f"bcr{pr}", "pe")
            bcn_done = P.now("act")

            sc_state = {"ld": [], "p1": 0, "p2": 0}

            def emit_scale_load(which, p):
                buf = (p if which == 1 else NPC1 + p) % 3
                src = d_w1 if which == 1 else d_w2
                G.use(f"wsc{buf}", "gpsimd")
                v = P.op("gpsimd", lambda e, p=p, buf=buf, src=src: e.dma_start(
                    out=wsc_b[buf][:], in_=src[:, p * PCW:(p + 1) * PCW]),
                    inc=(f"s_wsc{buf}", 16))
                sc_state["ld"].append((which, p, buf, v))

            def emit_scale_compute():
                if not sc_state["ld"]:
                    return False
                which, p, buf, v = sc_state["ld"].pop(0)
                P.wait("scalar", f"s_wsc{buf}", v)
                dst = scpart1 if which == 1 else scpart2
                P.op("scalar", lambda e, buf=buf, p=p, dst=dst:
                     e.activation(
                         out=wsc_b[buf][:], in_=wsc_b[buf][:], func=AF.Abs,
                         accum_out=dst[:, p:p + 1]), inc="act")
                ab_v = P.now("act")
                G.consumed(f"wsc{buf}", "act", ab_v)
                if which == 1:
                    sc_state["p1"] = p + 1
                    if p + 1 == NPC1:
                        sc_state["w1_act"] = ab_v
                else:
                    sc_state["p2"] = p + 1
                    if p + 1 == NPC2:
                        sc_state["w2_act"] = ab_v
                return True

            sc_next = [0, 0]
            def queue_next_load():
                if sc_next[0] < NPC1:
                    emit_scale_load(1, sc_next[0])
                    sc_next[0] += 1
                elif sc_next[1] < NPC2:
                    emit_scale_load(2, sc_next[1])
                    sc_next[1] += 1
            for _ in range(3):
                queue_next_load()

            def pump_scale(nsteps):
                for _ in range(nsteps):
                    if not emit_scale_compute():
                        return
                    queue_next_load()

            tsc = srow2[0:1, 2 * TO + 2 * 512:2 * TO + 2 * 512 + 2]
            mrow = srow2[0:1, 2 * TO + 2 * 512 + 2:2 * TO + 2 * 512 + 8]
            thr_ready = {}

            def emit_thr(j):
                P.vforce = True
                part = scpart1 if j == 0 else scpart2
                acc_done = sc_state["w1_act"] if j == 0 else sc_state["w2_act"]
                sl = srow2[0:1, 2 * TO + j * 128:2 * TO + (j + 1) * 128]
                sl2 = srow2[0:1, 2 * TO + 256 + j * 128:2 * TO + 256 + (j + 1) * 128]
                P.wait("vector", "act", acc_done)
                P.op("vector", lambda e: e.tensor_reduce(
                    out=spcol[:, j:j + 1], in_=part[:], axis=mybir.AxisListType.X,
                    op=ALU.add), inc="dve")
                spc_v = P.now("dve")
                P.wait("sync", "dve", spc_v)
                P.op("sync", lambda e: e.dma_start(
                    out=d_scp[:, j:j + 1], in_=spcol[:, j:j + 1],
                    allow_slow_non_contiguous=True), inc=("s_thr", 16))
                P.wait("sync", "s_thr", P.now("s_thr"))
                P.op("sync", lambda e: e.dma_start(
                    out=sl,
                    in_=d_scp[:, j:j + 1].rearrange("p n -> (p n)")[None, :],
                    allow_slow_non_contiguous=True), inc=("s_thr", 16))
                P.wait("vector", "s_thr", P.now("s_thr"))
                P.op("vector", lambda e: e.tensor_reduce(
                    out=mrow[:, j:j + 1], in_=sl, axis=mybir.AxisListType.X,
                    op=ALU.add))
                P.op("vector", lambda e: e.tensor_scalar(
                    out=mrow[:, j:j + 1], in0=mrow[:, j:j + 1],
                    scalar1=1.0 / 128, scalar2=None, op0=ALU.mult))
                P.op("vector", lambda e: e.tensor_scalar(
                    out=sl2, in0=sl, scalar1=mrow[:, j:j + 1], scalar2=None,
                    op0=ALU.subtract))
                P.op("vector", lambda e: e.tensor_reduce(
                    out=mrow[:, 2 + j:3 + j], in_=sl2, axis=mybir.AxisListType.X,
                    op=ALU.add))
                P.op("vector", lambda e: e.tensor_scalar(
                    out=tsc[:, j:j + 1], in0=mrow[:, j:j + 1], scalar1=128.0,
                    scalar2=mrow[:, 2 + j:3 + j], op0=ALU.mult, op1=ALU.add))
                P.op("vector", lambda e: e.tensor_scalar(
                    out=tsc[:, j:j + 1], in0=tsc[:, j:j + 1], scalar1=1.0 / NW,
                    scalar2=cfg["W_EPS"], op0=ALU.mult, op1=ALU.max))
                P.op("vector", lambda e: e.tensor_scalar(
                    out=tsc[:, j:j + 1], in0=tsc[:, j:j + 1], scalar1=0.5,
                    scalar2=None, op0=ALU.mult), inc="dve")
                thr_v = P.now("dve")
                P.wait("sync", "dve", thr_v)
                P.op("sync", lambda e: e.dma_start(
                    out=d_thr[0:1, j:j + 1], in_=tsc[:, j:j + 1],
                    allow_slow_non_contiguous=True), inc=("s_thr", 16))
                P.wait("sync", "s_thr", P.now("s_thr"))
                P.op("sync", lambda e: e.dma_start(
                    out=thr_cols[:, 2 * j:2 * j + 1],
                    in_=d_thr[0:1, j:j + 1].partition_broadcast(128)),
                    inc=("s_thr", 16))
                tb = P.now("s_thr")
                P.wait("vector", "s_thr", tb)
                P.op("vector", lambda e: e.tensor_scalar(
                    out=thr_cols[:, 2 * j + 1:2 * j + 2],
                    in0=thr_cols[:, 2 * j:2 * j + 1], scalar1=-1.0,
                    scalar2=None, op0=ALU.mult), inc="dve")
                thr_ready[j] = P.now("dve")
                P.vforce = False

            pump_per_dt = (NPC1 + NPC2 + DT - 1) // DT

            def pump_point(state):
                if state["budget"] > 0:
                    pump_scale(1)
                    state["budget"] -= 1
                if "w1_act" in sc_state and 0 not in thr_ready:
                    emit_thr(0)
            P.wait("sync", "s_xlw", xlw_all)
            xl_rd = {}
            def emit_xln_read(dt):
                G.use(f"xlt{dt % 2}", "sync")
                xl_rd[dt] = P.op("sync", lambda e, dt=dt: e.dma_start(
                    out=xln_t[dt % 2][:], in_=d_xln[dt * 128:(dt + 1) * 128, :]),
                    inc=(f"s_xl{dt % 2}", 16))
            for dt in range(min(2, DT)):
                emit_xln_read(dt)

            delta_done = {}
            def emit_delta_chunk(dt, c):
                dp = dt % 2
                b = c % 2
                G.use(f"bank{b}", "tensor")
                v = None
                for k in range(RK):
                    v = P.op("tensor", lambda e, k=k, c=c, dt=dt, b=b:
                             e.matmul(bank(b, cols=TC),
                                      wdt3[:, k, dt * 128:(dt + 1) * 128],
                                      dtr3[:, k, c * TC:(c + 1) * TC],
                                      start=(k == 0), stop=(k == RK - 1)),
                             inc="pe" if k == RK - 1 else None)
                P.wait("scalar", "pe", v)
                if c == 0:
                    G.use(f"delta{dp}", "scalar")
                P.op("scalar", lambda e, c=c, dt=dt, b=b, dp=dp: e.activation(
                    out=delta_t[dp][:, c * TC:(c + 1) * TC],
                    in_=bank(b, cols=TC),
                    func=AF.Relu, bias=bdt_c[:, dt:dt + 1], scale=1.0))
                G.use(f"sp{c % 2}", "scalar")
                P.op("scalar", lambda e, c=c, dt=dt, b=b: e.activation(
                    out=sp_t[c % 2][:], in_=bank(b, cols=TC), func=AF.Abs,
                    bias=bdt_c[:, dt:dt + 1], scale=1.0), inc="act")
                G.consumed(f"bank{b}", "act")
                P.op("scalar", lambda e, c=c: e.activation(
                    out=sp_t[c % 2][:], in_=sp_t[c % 2][:], func=AF.Exp,
                    scale=-1.0))
                P.op("scalar", lambda e, c=c: e.activation(
                    out=sp_t[c % 2][:], in_=sp_t[c % 2][:], func=AF.Ln,
                    bias=cvals[:, 1:2], scale=1.0), inc="act")
                spl_v = P.now("act")
                P.wait("vector", "act", spl_v)
                P.op("vector", lambda e, c=c, dp=dp: e.tensor_tensor(
                    out=delta_t[dp][:, c * TC:(c + 1) * TC],
                    in0=delta_t[dp][:, c * TC:(c + 1) * TC], in1=sp_t[c % 2][:],
                    op=ALU.add), inc="dve")
                G.consumed(f"sp{c % 2}", "dve")
                if c == NC4 - 1:
                    delta_done[dt] = P.now("dve")

            def emit_u(dt):
                dp = dt % 2
                P.wait("vector", f"s_xl{dp}", xl_rd[dt])
                G.use(f"u{dp}", "vector")
                P.op("vector", lambda e, dp=dp: e.tensor_tensor(
                    out=u_t[dp][:], in0=delta_t[dp][:], in1=xln_t[dp][:],
                    op=ALU.mult), inc="dve")

            voff("p4")
            for c in range(NC4):
                emit_delta_chunk(0, c)
            emit_u(0)
            for dt in range(DT):
                dp = dt % 2
                pstate = {"budget": pump_per_dt}
                G.use("xo_t", "sync")
                xo_v = P.op("sync", lambda e, dt=dt: e.dma_start(
                    out=xo_t[:], in_=d_x[dt * 128:(dt + 1) * 128, O0:T]),
                    inc=("s_xo", 16))
                if dt == 0:
                    ckpt("p4a")
                for n in range(N):
                    nb = n % 2
                    G.use(f"dA{nb}", "scalar")
                    P.wait("scalar", "dve", delta_done[dt])
                    av = P.op("scalar", lambda e, n=n, dt=dt, nb=nb, dp=dp:
                              e.activation(
                                  out=dA_b[nb][:], in_=delta_t[dp][:],
                                  func=AF.Exp,
                                  scale=negA[:, dt * N + n:dt * N + n + 1]),
                              inc="act")
                    P.op("vector", lambda e, n=n, dp=dp: e.tensor_tensor(
                        out=dbu_t[:], in0=u_t[dp][:], in1=B_bc[n][:],
                        op=ALU.mult))
                    P.wait("vector", "act", av)
                    P.vforce = True
                    P.op("vector", lambda e, nb=nb: e.tensor_tensor_scan(
                        out=h_t[:], data0=dA_b[nb][:], data1=dbu_t[:],
                        initial=0.0, op0=ALU.mult, op1=ALU.add), inc="dve")
                    P.vforce = False
                    G.consumed(f"dA{nb}", "dve")
                    if n == 0:
                        P.op("vector", lambda e, n=n: e.tensor_tensor(
                            out=y_t[:], in0=h_t[:, O0:T], in1=C_bc[n][:],
                            op=ALU.mult))
                    else:
                        P.op("vector", lambda e, n=n: e.tensor_tensor(
                            out=ty_t, in0=h_t[:, O0:T], in1=C_bc[n][:],
                            op=ALU.mult))
                        P.op("vector", lambda e: e.tensor_tensor(
                            out=y_t[:], in0=y_t[:], in1=ty_t, op=ALU.add))
                    if dt + 1 < DT and n < NC4:
                        emit_delta_chunk(dt + 1, n)
                    if dt + 1 < DT and n == NC4:
                        emit_u(dt + 1)
                    if n in (0, 2, 4, 6):
                        pump_point(pstate)
                G.consumed(f"delta{dp}", "act")
                G.consumed(f"u{dp}", "dve")
                if dt == 0:
                    ckpt("p4b")
                P.op("vector", lambda e, dt=dt, dp=dp: e.scalar_tensor_tensor(
                    out=y_t[:], in0=xln_t[dp][:, O0:T],
                    scalar=dpar_c[:, dt:dt + 1], in1=y_t[:],
                    op0=ALU.mult, op1=ALU.add))
                G.consumed(f"xlt{dp}", "dve")
                if dt + 2 < DT:
                    emit_xln_read(dt + 2)
                P.wait("vector", "s_xo", xo_v)
                G.use("x2_t", "vector")
                P.op("vector", lambda e: e.tensor_tensor(
                    out=x2_t[:], in0=xo_t[:], in1=y_t[:], op=ALU.add),
                    inc="dve")
                x2_v = P.now("dve")
                G.consumed("xo_t", "dve")
                P.wait("sync", "dve", x2_v)
                P.op("sync", lambda e, dt=dt: e.dma_start(
                    out=d_x2[dt * 128:(dt + 1) * 128, :], in_=x2_t[:]),
                    inc=("s_x2w", 16))
                G.consumed("x2_t", "s_x2w")
                G.use("x2bf_t", "vector")
                P.op("vector", lambda e: e.tensor_copy(x2bf_t[:], x2_t[:]))
                G.use("sq2", "vector")
                P.op("vector", lambda e: e.tensor_tensor(
                    out=sq2_t[:], in0=x2bf_t[:], in1=x2bf_t[:], op=ALU.mult),
                    inc="dve")
                s2_done = P.now("dve")
                P.wait("tensor", "dve", s2_done)
                if dt == 0:
                    for b in range(4, 8):
                        G.use(f"bank{b}", "tensor")
                for c in range(NC2):
                    P.op("tensor", lambda e, c=c, dt=dt: e.matmul(
                        bank(4 + c, rows=1, cols=TC), ones_cb[:],
                        x2bf_t[:, c * TC:(c + 1) * TC],
                        start=(dt == 0), stop=(dt == DT - 1)))
                for c in range(NC2):
                    P.op("tensor", lambda e, c=c, dt=dt: e.matmul(
                        bank(4 + NC2 + c, rows=1, cols=TC), ones_cb[:],
                        sq2_t[:, c * TC:(c + 1) * TC],
                        start=(dt == 0), stop=(dt == DT - 1)),
                        inc="pe" if c == NC2 - 1 else None)
                G.consumed("sq2", "pe")
                G.consumed("x2bf_t", "pe")
                if dt == 0:
                    ckpt("p4c")
                while pstate["budget"] > 0:
                    pump_point(pstate)
                if dt == 0:
                    ckpt("p4d")
            stats2_done = P.now("pe")
            von()
            ckpt("p4")
            pump_scale(NPC1 + NPC2)
            if 0 not in thr_ready:
                emit_thr(0)
            emit_thr(1)

            ckpt("thr", dump=(thr_cols[:], 128, 4))

            voff("ln2rows")
            s2x = srow2[0:1, 0:TO]
            s2q = srow2[0:1, TO:2 * TO]
            tmp2 = srow_bf[0:1, TO:2 * TO]
            P.wait("vector", "pe", stats2_done)
            for c in range(NC2):
                P.op("vector", lambda e, c=c: e.tensor_copy(
                    s2x[:, c * TC:(c + 1) * TC], bank(4 + c, rows=1, cols=TC)))
            for c in range(NC2):
                P.op("vector", lambda e, c=c: e.tensor_copy(
                    s2q[:, c * TC:(c + 1) * TC],
                    bank(4 + NC2 + c, rows=1, cols=TC)),
                    inc=("dve", 1) if c == NC2 - 1 else None)
            stats2_ev = P.now("dve")
            for b in range(4, 8):
                G.consumed(f"bank{b}", "dve", stats2_ev)
            P.op("vector", lambda e: e.tensor_scalar(
                out=s2x, in0=s2x, scalar1=1.0 / D, scalar2=None, op0=ALU.mult))
            P.op("vector", lambda e: e.tensor_scalar(
                out=s2q, in0=s2q, scalar1=1.0 / D, scalar2=None, op0=ALU.mult))
            P.op("vector", lambda e: e.tensor_tensor(
                out=tmp2, in0=s2x, in1=s2x, op=ALU.mult))
            P.op("vector", lambda e: e.tensor_tensor(
                out=s2q, in0=s2q, in1=tmp2, op=ALU.subtract), inc="dve")
            var2_done = P.now("dve")
            P.wait("scalar", "dve", var2_done)
            P.op("scalar", lambda e: e.activation(
                out=s2q, in_=s2q, func=AF.Ln, bias=cvals[0:1, 0:1], scale=1.0))
            P.op("scalar", lambda e: e.activation(
                out=s2q, in_=s2q, func=AF.Exp, scale=-0.5), inc="act")
            r2_done = P.now("act")
            P.wait("vector", "act", r2_done)
            P.op("vector", lambda e: e.tensor_tensor(
                out=s2x, in0=s2x, in1=s2q, op=ALU.mult))
            P.op("vector", lambda e: e.tensor_copy(srow_bf[0:1, 0:TO], s2q))
            P.op("vector", lambda e: e.tensor_copy(srow_bf[0:1, T:T + TO], s2x),
                 inc="dve")
            rows2_bf = P.now("dve")
            von()
            P.flush(nc)
            cctx.close()

        bctx = ExitStack()
        with bctx:
            def bsbt(name, shape, dt=F32):
                return bctx.enter_context(nc.sbuf_tensor(name, shape, dt))
            h_sb = bsbt("h_sb", [128, FT * TO], BF16)
            f1ctx = ExitStack()
            f1sbt = lambda name, shape, dt=F32: f1ctx.enter_context(
                nc.sbuf_tensor(name, shape, dt))
            xln2 = [f1sbt(f"xln2_{i}", [128, TO], BF16) for i in range(DT)]
            r2_bc = f1sbt("r2_bc", [128, TO], BF16)
            m2_bc = f1sbt("m2_bc", [128, TO], BF16)
            x2r_t = [f1sbt(f"x2rt{i}", [128, TO]) for i in range(2)]
            w1s_b = [f1sbt(f"w1sb{i}", [128, DT * 128]) for i in range(2)]
            w1q_b = [f1sbt(f"w1qb{i}", [128, DT * 128], BF16) for i in range(2)]

            w1u_v = {}
            SLW1 = DT * 128
            def emit_w1_load(ft):
                G.use(f"w1s{ft % 2}", "sync")
                w1u_v[ft] = P.op("sync", lambda e, ft=ft: e.dma_start(
                    out=w1s_b[ft % 2][:],
                    in_=d_w1[:, ft * SLW1:(ft + 1) * SLW1]),
                    inc=(f"s_w1{ft % 2}", 16))
            tern1_v = {}
            def emit_w1_tern(ft):
                voff("ffn1")
                fb = ft % 2
                P.wait("vector", f"s_w1{fb}", w1u_v[ft])
                G.use(f"w1q{fb}", "vector")
                P.op("vector", lambda e, fb=fb: e.tensor_scalar(
                    out=w1q_b[fb][:], in0=w1s_b[fb][:], scalar1=thr_cols[:, 1:2],
                    scalar2=None, op0=ALU.is_lt))
                P.op("vector", lambda e, fb=fb: e.scalar_tensor_tensor(
                    out=w1q_b[fb][:], in0=w1s_b[fb][:], scalar=thr_cols[:, 0:1],
                    in1=w1q_b[fb][:], op0=ALU.is_gt, op1=ALU.subtract), inc="dve")
                tern1_v[ft] = P.now("dve")
                von()
                G.consumed(f"w1s{fb}", "dve")
            for ft in range(min(2, FT)):
                emit_w1_load(ft)
            P.wait("vector", "dve", thr_ready[0])
            for ft in range(min(2, FT)):
                emit_w1_tern(ft)

            P.wait("tensor", "dve", rows2_bf)
            k1_bcast(r2_bc, srow_bf[0:1, 0:TO], NC2, bank_pair=(0, 1))
            k1_bcast(m2_bc, srow_bf[0:1, T:T + TO], NC2, bank_pair=(0, 1))
            bc2_done = P.now("act")

            P.wait("vector", "act", bc2_done)
            voff("xln2")
            x2r_v = {}
            def emit_x2r(i):
                G.use(f"x2r{i % 2}", "sync")
                P.wait("sync", "s_x2w", 16 * (i + 1))
                x2r_v[i] = P.op("sync", lambda e, i=i: e.dma_start(
                    out=x2r_t[i % 2][:], in_=d_x2[i * 128:(i + 1) * 128, :]),
                    inc=(f"s_x2r{i % 2}", 16))
            for i in range(min(2, DT)):
                emit_x2r(i)
            xln2_v = {}
            for i in range(DT):
                P.wait("vector", f"s_x2r{i % 2}", x2r_v[i])
                P.op("vector", lambda e, i=i: e.tensor_tensor(
                    out=xln2[i][:], in0=x2r_t[i % 2][:], in1=r2_bc[:],
                    op=ALU.mult))
                P.op("vector", lambda e, i=i: e.tensor_tensor(
                    out=xln2[i][:], in0=xln2[i][:], in1=m2_bc[:], op=ALU.subtract),
                    inc="dve")
                xln2_v[i] = P.now("dve")
                G.consumed(f"x2r{i % 2}", "dve")
                if i + 2 < DT:
                    emit_x2r(i + 2)
            xln2_done = P.now("dve")
            von()
            ckpt("xln2")

            for ft in range(FT):
                fb = ft % 2
                P.wait("tensor", "dve", tern1_v[ft])
                if ft == 0:
                    P.wait("tensor", "act", bc2_done)
                for c in range(NC2):
                    b = (ft * NC2 + c) % 2
                    G.use(f"bank{b}", "tensor")
                    v = None
                    for k in range(DT):
                        if ft == 0:
                            P.wait("tensor", "dve", xln2_v[k])
                        v = P.op("tensor", lambda e, k=k, c=c, fb=fb, b=b:
                                 e.matmul(bank(b, cols=TC),
                                          w1q_b[fb][:, k * 128:(k + 1) * 128],
                                          xln2[k][:, c * TC:(c + 1) * TC],
                                          start=(k == 0), stop=(k == DT - 1)),
                                 inc="pe" if k == DT - 1 else None)
                    P.wait("scalar", "pe", v)
                    P.op("scalar", lambda e, ft=ft, c=c, b=b: e.activation(
                        out=h_sb[:, ft * TO + c * TC: ft * TO + (c + 1) * TC],
                        in_=bank(b, cols=TC), func=AF.Gelu,
                        bias=b1_c[:, ft:ft + 1], scale=1.0), inc="act")
                    G.consumed(f"bank{b}", "act")
                G.consumed(f"w1q{fb}", "pe")
                if ft + 2 < FT:
                    emit_w1_load(ft + 2)
                    emit_w1_tern(ft + 2)
            ffn1_done = P.now("act")
            ckpt("ffn1")
            P.flush(nc)
            f1ctx.close()

            f2ctx = bctx.enter_context(ExitStack())
            f2sbt = lambda name, shape, dt=F32: f2ctx.enter_context(
                nc.sbuf_tensor(name, shape, dt))
            w2s_b = [f2sbt(f"w2sb{i}", [128, KSK * DQW]) for i in range(4)]
            w2q_b = [f2sbt(f"w2qb{i}", [128, KSK * DQW], BF16) for i in range(2)]
            oev_b = [f2sbt(f"oevb{i}", [128, TC]) for i in range(2)]
            x2e_b = [f2sbt(f"x2eb{i}", [128, TC]) for i in range(4)]
            SLW2 = KSK * DQW
            w2u_v = {}
            def emit_w2_load(idx):
                q = "sync" if idx % 2 == 0 else "scalar"
                G.use(f"w2s{idx % 4}", q)
                w2u_v[idx] = P.op(q, lambda e, idx=idx: e.dma_start(
                    out=w2s_b[idx % 4][:],
                    in_=d_w2[:, idx * SLW2:(idx + 1) * SLW2]),
                    inc=(f"s_w2s{idx % 4}", 16), dma=True)
            for idx in range(min(4, DQ * NKS)):
                emit_w2_load(idx)
            P.wait("tensor", "act", ffn1_done)
            P.wait("vector", "dve", thr_ready[1])
            voff("ffn2")

            x2e_v = {}
            def emit_x2e_loads(dq):
                for j in range(DQT * NC2):
                    ck = dq * DQT * NC2 + j
                    gdt = dq * DQT + j // NC2
                    c = j % NC2
                    eb = ck % 4
                    G.use(f"x2eb{eb}", "sync")
                    x2e_v[ck] = P.op("sync", lambda e, gdt=gdt, c=c, eb=eb:
                                     e.dma_start(
                                         out=x2e_b[eb][:],
                                         in_=d_x2[gdt * 128:(gdt + 1) * 128,
                                                  c * TC:(c + 1) * TC]),
                                     inc=(f"s_x2e{eb}", 16))
            evst = {"chunk": 0}
            def emit_evacs(dq, mmv):
                base = (dq % 2) * 4 if DQ > 1 else 0
                for dtq in range(DQT):
                    gdt = dq * DQT + dtq
                    for c in range(NC2):
                        ck = evst["chunk"]
                        eb = ck % 4
                        evst["chunk"] += 1
                        ob = ck % 2
                        P.wait("vector", "pe", mmv)
                        P.wait("vector", f"s_x2e{eb}", x2e_v[ck])
                        G.use(f"oev{ob}", "vector")
                        bnum = base + dtq * NC2 + c
                        P.op("vector", lambda e, gdt=gdt, bnum=bnum, eb=eb,
                             ob=ob:
                             e.scalar_tensor_tensor(
                                 out=oev_b[ob][:],
                                 in0=ps[0:128, bnum * 512:bnum * 512 + TC],
                                 scalar=b2_c[:, gdt:gdt + 1], in1=x2e_b[eb][:],
                                 op0=ALU.add, op1=ALU.add), inc="dve")
                        ev = P.now("dve")
                        G.consumed(f"x2eb{eb}", "dve", ev)
                        G.consumed(f"bank{bnum}", "dve", ev)
                        P.wait("sync", "dve", ev)
                        P.op("sync", lambda e, gdt=gdt, c=c, ob=ob: e.dma_start(
                            out=d_out[gdt * 128:(gdt + 1) * 128,
                                      c * TC:(c + 1) * TC],
                            in_=oev_b[ob][:]), inc=(f"s_out{ob}", 16))
                        G.consumed(f"oev{ob}", f"s_out{ob}")

            emit_x2e_loads(0)
            pending = None
            for dq in range(DQ):
                base = (dq % 2) * 4 if DQ > 1 else 0
                for b in range(base, base + DQT * NC2):
                    G.use(f"bank{b}", "tensor")
                for ks in range(NKS):
                    idx = dq * NKS + ks
                    sb4 = idx % 4
                    sb2 = idx % 2
                    P.wait("vector", f"s_w2s{sb4}", w2u_v[idx])
                    G.use(f"w2q{sb2}", "vector")
                    P.op("vector", lambda e, sb2=sb2, sb4=sb4: e.tensor_scalar(
                        out=w2q_b[sb2][:], in0=w2s_b[sb4][:],
                        scalar1=thr_cols[:, 3:4],
                        scalar2=None, op0=ALU.is_lt))
                    P.op("vector", lambda e, sb2=sb2, sb4=sb4:
                         e.scalar_tensor_tensor(
                             out=w2q_b[sb2][:], in0=w2s_b[sb4][:],
                             scalar=thr_cols[:, 2:3], in1=w2q_b[sb2][:],
                             op0=ALU.is_gt, op1=ALU.subtract), inc="dve")
                    tern2_v = P.now("dve")
                    G.consumed(f"w2s{sb4}", "dve")
                    if ks == min(1, NKS - 1) and pending is not None:
                        emit_evacs(*pending)
                        pending = None
                        emit_x2e_loads(dq)
                    P.wait("tensor", "dve", tern2_v)
                    for kk in range(KSK):
                        kglob = ks * KSK + kk
                        for dtq in range(DQT):
                            for c in range(NC2):
                                bnum = base + dtq * NC2 + c
                                lastk = (kk == KSK - 1 and dtq == DQT - 1
                                         and c == NC2 - 1)
                                P.op("tensor", lambda e, kk=kk, dtq=dtq, c=c,
                                     sb2=sb2, kglob=kglob, bnum=bnum, ks=ks:
                                     e.matmul(
                                         ps[0:128, bnum * 512:bnum * 512 + TC],
                                         w2q_b[sb2][:, kk * DQW + dtq * 128:
                                                    kk * DQW + (dtq + 1) * 128],
                                         h_sb[:, kglob * TO + c * TC:
                                              kglob * TO + (c + 1) * TC],
                                         start=(ks == 0 and kk == 0),
                                         stop=(ks == NKS - 1 and kk == KSK - 1)),
                                     inc="pe" if lastk else None)
                    G.consumed(f"w2q{sb2}", "pe")
                    if idx + 4 < DQ * NKS:
                        emit_w2_load(idx + 4)
                mm_done = P.now("pe")
                pending = (dq, mm_done)
            emit_evacs(*pending)
            von()
            P.wait("gpsimd", "s_out0", P.now("s_out0"))
            P.wait("gpsimd", "s_out1", P.now("s_out1"))
            P.flush(nc)
    except _StopBuild:
        pass
    return nc


_CACHE = {}

def _build_full():
    if "nc" not in _CACHE:
        _CACHE["nc"] = build_nc(make_cfg(full=True))
    return _CACHE["nc"]


def kernel(L0_attn_x, L0_attn_W_xproj, L0_attn_W_dt, L0_attn_b_dt, L0_attn_A_log,
           L0_attn_D_param, L0_ffn_w1, L0_ffn_b1, L0_ffn_w2, L0_ffn_b2):
    x = np.asarray(L0_attn_x, np.float32)
    Wxp = np.asarray(L0_attn_W_xproj, np.float32)
    Wdt = np.asarray(L0_attn_W_dt, np.float32)
    bdt = np.asarray(L0_attn_b_dt, np.float32)
    Alog = np.asarray(L0_attn_A_log, np.float32)
    Dpar = np.asarray(L0_attn_D_param, np.float32)
    w1 = np.asarray(L0_ffn_w1, np.float32)
    b1 = np.asarray(L0_ffn_b1, np.float32)
    w2 = np.asarray(L0_ffn_w2, np.float32)
    b2 = np.asarray(L0_ffn_b2, np.float32)

    B, L, D = x.shape
    TO = L // 2
    N = Alog.shape[1]
    DT = D // 128
    FT = w1.shape[0] // 128

    wxpT = np.ascontiguousarray(Wxp.T)
    wdtT = np.ascontiguousarray(Wdt.T)
    Fdim = w1.shape[0]
    DQW = min(256, D)
    w1T = np.ascontiguousarray(
        w1.T.reshape(DT, 128, FT, 128).transpose(1, 2, 0, 3).reshape(128, -1))
    w2T = np.ascontiguousarray(
        w2.T.reshape(Fdim // 128, 128, D // DQW, DQW)
        .transpose(1, 2, 0, 3).reshape(128, -1))
    bdt_l = np.ascontiguousarray(bdt.reshape(DT, 128).T)
    dpar_l = np.ascontiguousarray(Dpar.reshape(DT, 128).T)
    alog_l = np.ascontiguousarray(
        Alog.reshape(DT, 128, N).transpose(1, 0, 2).reshape(128, DT * N))
    b1_l = np.ascontiguousarray(b1.reshape(FT, 128).T)
    b2_l = np.ascontiguousarray(b2.reshape(DT, 128).T)

    in_maps = []
    for c in range(8):
        b, r = c // 2, c % 2
        xT = np.ascontiguousarray(x[b].T)
        own = xT[:, r * TO:(r + 1) * TO]
        prefix = np.zeros((D, TO), np.float32) if r == 0 else xT[:, 0:TO]
        x_c = np.ascontiguousarray(np.concatenate([prefix, own], axis=1))
        in_maps.append({
            "x": x_c, "wxp": wxpT, "wdt": wdtT, "bdt": bdt_l, "alog": alog_l,
            "dpar": dpar_l, "w1t": w1T, "b1": b1_l, "w2t": w2T, "b2": b2_l,
        })

    nc = _build_full()
    res = run_bass_kernel_spmd(nc, in_maps, core_ids=list(range(8)))

    out = np.empty((B, L, D), np.float32)
    for c in range(8):
        b, r = c // 2, c % 2
        out[b, r * TO:(r + 1) * TO, :] = res.results[c]["out"].T
    return out


# version 18
# speedup vs baseline: 1.0173x; 1.0173x over previous
import numpy as np
import ml_dtypes
import concourse.bass as bass
import concourse.mybir as mybir
from concourse.bass_utils import run_bass_kernel_spmd
from contextlib import ExitStack

F32 = mybir.dt.float32
BF16 = mybir.dt.bfloat16
FP16 = mybir.dt.float16
AF = mybir.ActivationFunctionType
ALU = mybir.AluOpType

ENGS = ("sync", "gpsimd", "vector", "scalar", "tensor")


def make_cfg(full=True):
    if full:
        D, T, TO, R, F, N, TC = 2048, 2048, 1024, 256, 8192, 8, 512
        SCP = 128
    else:
        D, T, TO, R, F, N, TC = 256, 256, 128, 128, 512, 8, 64
        SCP = 128
    cfg = dict(D=D, T=T, TO=TO, R=R, F=F, N=N, TC=TC, SCP=SCP)
    cfg["DT"] = D // 128
    cfg["FT"] = F // 128
    cfg["RK"] = R // 128
    cfg["NC4"] = T // TC
    cfg["NC2"] = TO // TC
    cfg["O0"] = T - TO
    cfg["DQW"] = min(256, D)
    cfg["DQ"] = D // cfg["DQW"]
    cfg["KS"] = min(512, F)
    cfg["LN_EPS"] = 1e-5
    cfg["W_EPS"] = 1e-8
    return cfg


class Prog:
    ESEM = {"vector": "dve", "scalar": "act", "gpsimd": "pool"}

    def __init__(self, paranoid=True, self_waits=False):
        self.ops = {e: [] for e in ENGS}
        self.cnt = {}
        self.sems = {}
        self.paranoid = paranoid
        if self_waits is True:
            self_waits = {"vector", "scalar", "gpsimd"}
        self.self_waits = set(self_waits) if self_waits else set()
        self.vforce = False
        self.voff = False
        self.vrelax = set()

    def sem(self, name):
        self.cnt.setdefault(name, 0)
        return name

    def op(self, eng, fn, inc=None, dma=False):
        esem = self.ESEM.get(eng)
        incs = []
        if inc is not None:
            if isinstance(inc, str):
                inc = (inc, 1)
            incs.append(inc)
        if (self.paranoid and not dma and esem is not None
                and eng in ("vector", "scalar")):
            if not any(n == esem for n, _ in incs):
                incs.append((esem, 1))
            if ((eng in self.self_waits and not (eng == "vector" and self.voff))
                    or (eng == "vector" and self.vforce)):
                self.wait(eng, esem, self.now(esem))
        if incs:
            for name, by in incs:
                self.sem(name)
                self.cnt[name] += by
            sems = self.sems
            def wrapped(e, fn=fn, incs=tuple(incs)):
                ins = fn(e)
                for name, by in incs:
                    ins = ins.then_inc(sems[name], by)
            self.ops[eng].append(wrapped)
            return self.cnt[inc[0]] if inc is not None else self.cnt[esem]
        self.ops[eng].append(fn)
        return None

    def wait(self, eng, sem_name, val):
        if val is None or val <= 0:
            return
        sems = self.sems
        self.ops[eng].append(lambda e, s=sem_name, v=val: e.wait_ge(sems[s], v))

    def now(self, name):
        return self.cnt.get(name, 0)

    def flush(self, nc):
        pending = {e: ops for e, ops in self.ops.items() if ops}
        if not pending:
            return
        with nc.Block() as blk:
            for eng_name, ops in pending.items():
                getattr(blk, eng_name)(lambda e, ops=ops: [f(e) for f in ops])
        self.ops = {e: [] for e in ENGS}


class Guard:
    def __init__(self, prog):
        self.P = prog
        self.m = {}

    def consumed(self, key, sem, val=None):
        self.m.setdefault(key, {})[sem] = self.P.now(sem) if val is None else val

    def use(self, key, eng):
        for sem, val in self.m.pop(key, {}).items():
            self.P.wait(eng, sem, val)


class _StopBuild(Exception):
    pass


def build_nc(cfg, stop=None, self_waits=("vector",),
             vrelax=("p1cast", "ln1rows", "ln2rows", "xln2", "ffn1", "ffn2",
                     "p4", "p0", "p3apply")):
    D, T, TO, R, F, N, TC, SCP = (cfg[k] for k in
                                  ("D", "T", "TO", "R", "F", "N", "TC", "SCP"))
    DT, FT, RK, NC4, NC2, O0 = (cfg[k] for k in
                                ("DT", "FT", "RK", "NC4", "NC2", "O0"))
    DQ, DQW, KS = cfg["DQ"], cfg["DQW"], cfg["KS"]
    NW = D * F
    FK = F // 128
    DQT = DQW // 128
    KSK = KS // 128
    NKS = F // KS
    WTOT = D * F // 128
    PCW = min(2048, WTOT)
    NPC1 = WTOT // PCW
    NPC2 = WTOT // PCW

    nc = bass.Bass("TRN2", target_bir_lowering=False)
    d_x = nc.dram_tensor("x", [D, T], F32, kind="ExternalInput").ap()
    d_wxp = nc.dram_tensor("wxp", [D, R + 2 * N], F32, kind="ExternalInput").ap()
    d_wdt = nc.dram_tensor("wdt", [R, D], F32, kind="ExternalInput").ap()
    d_bdt = nc.dram_tensor("bdt", [128, DT], F32, kind="ExternalInput").ap()
    d_alog = nc.dram_tensor("alog", [128, DT * N], F32, kind="ExternalInput").ap()
    d_dpar = nc.dram_tensor("dpar", [128, DT], F32, kind="ExternalInput").ap()
    d_w1 = nc.dram_tensor("w1t", [128, WTOT], F32, kind="ExternalInput").ap()
    d_b1 = nc.dram_tensor("b1", [128, FT], F32, kind="ExternalInput").ap()
    d_w2 = nc.dram_tensor("w2t", [128, WTOT], F32, kind="ExternalInput").ap()
    d_b2 = nc.dram_tensor("b2", [128, DT], F32, kind="ExternalInput").ap()
    d_out = nc.dram_tensor("out", [D, TO], F32, kind="ExternalOutput").ap()
    d_x2 = nc.dram_tensor("x2s", [D, TO], F32, kind="Internal").ap()
    d_xln = nc.dram_tensor("xlns", [D, T], BF16, kind="Internal").ap()
    d_thr = nc.dram_tensor("thrs", [1, 2], F32, kind="Internal").ap()
    d_scp = nc.dram_tensor("scps", [128, 2], F32, kind="Internal").ap()

    P = Prog(self_waits=self_waits)
    P.vrelax = set(vrelax)
    G = Guard(P)

    def voff(name):
        P.voff = name in P.vrelax

    def von():
        P.voff = False

    def ckpt(tag, dump=None):
        if stop == tag:
            if dump is not None:
                P.wait("sync", "dve", P.now("dve"))
                P.op("sync", lambda e: e.dma_start(
                    out=d_out[0:dump[1], 0:dump[2]], in_=dump[0]),
                    inc=("s_out0", 16))
                P.wait("gpsimd", "s_out0", P.now("s_out0"))
            P.wait("gpsimd", "dve", P.now("dve"))
            P.wait("gpsimd", "act", P.now("act"))
            P.wait("gpsimd", "pe", P.now("pe"))
            P.flush(nc)
            raise _StopBuild

    ctx = ExitStack()
    try:
      with ctx:
        def sbt(name, shape, dt=F32):
            return ctx.enter_context(nc.sbuf_tensor(name, shape, dt))

        bdt_c = sbt("bdt_c", [128, DT])
        negA = sbt("negA", [128, DT * N])
        dpar_c = sbt("dpar_c", [128, DT])
        b1_c = sbt("b1_c", [128, FT])
        b2_c = sbt("b2_c", [128, DT])
        ones_cb = sbt("ones_cb", [128, 1], BF16)
        ones_rb = sbt("ones_rb", [1, 128], BF16)
        cvals = sbt("cvals", [128, 2])
        srow_bf = sbt("srow_bf", [1, 2 * T], BF16)
        thr_cols = sbt("thr_cols", [128, 4])

        ps = ctx.enter_context(nc.psum_tensor("ps", [128, 4096], F32))
        def bank(b, rows=128, cols=512, c0=0):
            return ps[0:rows, b * 512 + c0: b * 512 + c0 + cols]

        for s in ("s_x0", "s_x1", "s_x2", "s_x3", "s_sm", "s_smg", "s_smg2",
                  "s_wsc0", "s_wsc1", "s_wsc2", "s_w10", "s_w11",
                  "s_w2s0", "s_w2s1", "s_w2s2", "s_w2s3",
                  "s_xo", "s_x2w", "s_x2r0", "s_x2r1", "s_x2e0", "s_x2e1",
                  "s_x2e2", "s_x2e3",
                  "s_out0", "s_out1", "s_br0", "s_br1", "s_thr", "s_xlw", "s_xl0", "s_xl1",
                  "pe", "act", "dve", "pool"):
            P.sem(s)
            P.sems[s] = ctx.enter_context(nc.semaphore(s))

        for dst, src in ((bdt_c, d_bdt), (dpar_c, d_dpar), (b1_c, d_b1),
                         (b2_c, d_b2), (negA, d_alog)):
            P.op("sync", lambda e, dd=dst, ss=src: e.dma_start(out=dd[:], in_=ss),
                 inc=("s_sm", 16))
        sm_all = P.now("s_sm")
        P.wait("scalar", "s_sm", sm_all)
        P.op("scalar", lambda e: e.activation(out=negA[:], in_=negA[:],
                                              func=AF.Exp), inc="act")
        aexp_done = P.now("act")
        P.wait("vector", "act", aexp_done)
        voff("p0")
        P.op("vector", lambda e: e.tensor_scalar(
            out=negA[:], in0=negA[:], scalar1=-1.0, scalar2=None, op0=ALU.mult))
        P.op("vector", lambda e: e.memset(ones_cb[:], 1.0))
        P.op("vector", lambda e: e.memset(ones_rb[:], 1.0))
        P.op("vector", lambda e: e.memset(cvals[:, 0:1], 1e-5))
        P.op("vector", lambda e: e.memset(cvals[:, 1:2], 1.0), inc="dve")
        setup_done = P.now("dve")
        von()

        actx = ExitStack()
        with actx:
            def asbt(name, shape, dt=F32):
                return actx.enter_context(nc.sbuf_tensor(name, shape, dt))
            wdt_bf = asbt("wdt_bf", [128, RK * D], BF16)
            dtr_bf = asbt("dtr_bf", [128, RK * T], BF16)
            bc_rows = asbt("bc_rows", [1, 2 * T], BF16)
            BC_sb = asbt("BC_sb", [32 + N, T])
            ones_f16 = asbt("ones_f16", [128, 1], FP16)
            ones_cf = asbt("ones_cf", [128, 1])
            P.op("vector", lambda e: e.memset(ones_cf[:], 1.0))
            P.op("vector", lambda e: e.memset(ones_f16[:], 1.0), inc="dve")
            ones_sc_done = P.now("dve")
            scpart1 = asbt("scpart1", [128, NPC1])
            scpart2 = asbt("scpart2", [128, NPC2])
            spcol = asbt("spcol", [128, 2])
            dtr3 = dtr_bf[:].rearrange("p (k t) -> p k t", k=RK)
            wdt3 = wdt_bf[:].rearrange("p (k m) -> p k m", k=RK)
            P.op("gpsimd", lambda e: e.dma_start(
                out=wdt3, in_=d_wdt.rearrange("(k p) m -> p k m", p=128)),
                inc=("s_smg", 16))
            smg_all = P.now("s_smg")

            abctx = ExitStack()
            absbt = lambda name, shape, dt=F32: abctx.enter_context(
                nc.sbuf_tensor(name, shape, dt))
            wxp_bf = absbt("wxp_bf", [128, DT * (R + 2 * N)], BF16)
            srow = absbt("srow", [1, 2 * T + 64])
            xbf = [absbt(f"xbf{i}", [128, T], BF16) for i in range(DT)]
            sqt = absbt("sqt", [128, T], BF16)
            xst = [absbt(f"xst{i}", [128, T]) for i in range(4)]
            rstd_bc = absbt("rstd_bc", [128, T], BF16)
            mr_bc = absbt("mr_bc", [128, T], BF16)
            lnt = sqt
            wxp3 = wxp_bf[:].rearrange("p (k m) -> p k m", k=DT)
            P.op("gpsimd", lambda e: e.dma_start(
                out=wxp3, in_=d_wxp.rearrange("(k p) m -> p k m", p=128)),
                inc=("s_smg2", 16))
            smg2_all = P.now("s_smg2")

            x_ld = {}
            def emit_x_load(i):
                q = "sync" if i % 2 == 0 else "scalar"
                G.use(f"xst{i % 4}", q)
                x_ld[i] = P.op(q, lambda e, i=i: e.dma_start(
                    out=xst[i % 4][:], in_=d_x[i * 128:(i + 1) * 128, :]),
                    inc=(f"s_x{i % 4}", 16), dma=True)
            for i in range(min(4, DT)):
                emit_x_load(i)
            P.wait("tensor", "dve", setup_done)
            voff("p1cast")
            for i in range(DT):
                P.wait("vector", f"s_x{i % 4}", x_ld[i])
                P.op("vector", lambda e, i=i: e.tensor_copy(
                    xbf[i][:], xst[i % 4][:]), inc="dve")
                G.consumed(f"xst{i % 4}", "dve")
                if i + 4 < DT:
                    emit_x_load(i + 4)
                G.use("sqt", "vector")
                P.op("vector", lambda e, i=i: e.tensor_tensor(
                    out=sqt[:], in0=xbf[i][:], in1=xbf[i][:], op=ALU.mult),
                    inc="dve")
                sq_done = P.now("dve")
                P.wait("tensor", "dve", sq_done)
                for c in range(NC4):
                    P.op("tensor", lambda e, i=i, c=c: e.matmul(
                        bank(c, rows=1, cols=TC), ones_cb[:],
                        xbf[i][:, c * TC:(c + 1) * TC],
                        start=(i == 0), stop=(i == DT - 1)))
                for c in range(NC4):
                    P.op("tensor", lambda e, i=i, c=c: e.matmul(
                        bank(NC4 + c, rows=1, cols=TC), ones_cb[:],
                        sqt[:, c * TC:(c + 1) * TC],
                        start=(i == 0), stop=(i == DT - 1)),
                        inc="pe" if c == NC4 - 1 else None)
                G.consumed("sqt", "pe")
            stats1_done = P.now("pe")
            von()
            ckpt("p1")

            voff("ln1rows")
            sx = srow[0:1, 0:T]
            sq_r = srow[0:1, T:2 * T]
            tmp_r = srow_bf[0:1, 0:T]
            P.wait("vector", "pe", stats1_done)
            for c in range(NC4):
                P.op("vector", lambda e, c=c: e.tensor_copy(
                    sx[:, c * TC:(c + 1) * TC], bank(c, rows=1, cols=TC)))
            for c in range(NC4):
                P.op("vector", lambda e, c=c: e.tensor_copy(
                    sq_r[:, c * TC:(c + 1) * TC], bank(NC4 + c, rows=1, cols=TC)),
                    inc=("dve", 1) if c == NC4 - 1 else None)
            stats1_ev = P.now("dve")
            for b in range(8):
                G.consumed(f"bank{b}", "dve", stats1_ev)
            P.op("vector", lambda e: e.tensor_scalar(
                out=sx, in0=sx, scalar1=1.0 / D, scalar2=None, op0=ALU.mult))
            P.op("vector", lambda e: e.tensor_scalar(
                out=sq_r, in0=sq_r, scalar1=1.0 / D, scalar2=None, op0=ALU.mult))
            P.op("vector", lambda e: e.tensor_tensor(
                out=tmp_r, in0=sx, in1=sx, op=ALU.mult))
            P.op("vector", lambda e: e.tensor_tensor(
                out=sq_r, in0=sq_r, in1=tmp_r, op=ALU.subtract), inc="dve")
            var_done = P.now("dve")
            P.wait("scalar", "dve", var_done)
            P.op("scalar", lambda e: e.activation(
                out=sq_r, in_=sq_r, func=AF.Ln, bias=cvals[0:1, 0:1], scale=1.0))
            P.op("scalar", lambda e: e.activation(
                out=sq_r, in_=sq_r, func=AF.Exp, scale=-0.5), inc="act")
            rstd_done = P.now("act")
            P.wait("vector", "act", rstd_done)
            P.op("vector", lambda e: e.tensor_tensor(
                out=sx, in0=sx, in1=sq_r, op=ALU.mult))
            rrow_bf = srow_bf[0:1, 0:T]
            mrow_bf = srow_bf[0:1, T:2 * T]
            P.op("vector", lambda e: e.tensor_copy(rrow_bf, sq_r))
            P.op("vector", lambda e: e.tensor_copy(mrow_bf, sx), inc="dve")
            rows_bf_done = P.now("dve")
            P.wait("tensor", "dve", rows_bf_done)

            def k1_bcast(dst, row_ap, nch, bank_pair=(0, 1, 2, 3)):
                assert nch <= len(bank_pair)
                v = None
                for c in range(nch):
                    b = bank_pair[c % len(bank_pair)]
                    G.use(f"bank{b}", "tensor")
                    v = P.op("tensor", lambda e, c=c, b=b, row_ap=row_ap:
                             e.matmul(bank(b, cols=TC), ones_rb[0:1, :],
                                      row_ap[:, c * TC:(c + 1) * TC],
                                      start=True, stop=True), inc="pe")
                P.wait("scalar", "pe", v)
                for c in range(nch):
                    b = bank_pair[c % len(bank_pair)]
                    P.op("scalar", lambda e, c=c, b=b, dst=dst: e.activation(
                        out=dst[:, c * TC:(c + 1) * TC], in_=bank(b, cols=TC),
                        func=AF.Copy), inc="act")
                    G.consumed(f"bank{b}", "act")

            k1_bcast(rstd_bc, rrow_bf, NC4)
            k1_bcast(mr_bc, mrow_bf, NC4)
            bc1_done = P.now("act")
            von()
            ckpt("p2")

            P.wait("vector", "act", bc1_done)
            voff("p3apply")
            xap_v = {}
            for i in range(DT):
                P.op("vector", lambda e, i=i: e.tensor_tensor(
                    out=lnt[:], in0=xbf[i][:], in1=rstd_bc[:], op=ALU.mult))
                P.op("vector", lambda e, i=i: e.tensor_tensor(
                    out=xbf[i][:], in0=lnt[:], in1=mr_bc[:], op=ALU.subtract),
                    inc="dve")
                xl_v = P.now("dve")
                xap_v[i] = xl_v
                P.wait("sync", "dve", xl_v)
                P.op("sync", lambda e, i=i: e.dma_start(
                    out=d_xln[i * 128:(i + 1) * 128, :], in_=xbf[i][:]),
                    inc=("s_xlw", 16))
            xln_done = P.now("dve")
            von()
            xlw_all = P.now("s_xlw")

            P.wait("tensor", "s_sm", sm_all)
            P.wait("tensor", "s_smg", smg_all)
            P.wait("tensor", "s_smg2", smg2_all)
            mchunks = [("dtr", mi, mi * 128, 128) for mi in range(RK)] + \
                      [("B", 0, R, N), ("C", 1, R + N, N)]
            first_mchunk = [True]
            for (kind, mi, m0, msz) in mchunks:
                for c in range(NC4):
                    b = c % 2
                    G.use(f"bank{b}", "tensor")
                    v = None
                    for k in range(DT):
                        if first_mchunk[0]:
                            P.wait("tensor", "dve", xap_v[k])
                        v = P.op("tensor", lambda e, k=k, c=c, m0=m0, msz=msz, b=b:
                                 e.matmul(bank(b, rows=msz, cols=TC),
                                          wxp3[:, k, m0:m0 + msz],
                                          xbf[k][:, c * TC:(c + 1) * TC],
                                          start=(k == 0), stop=(k == DT - 1)),
                                 inc="pe" if k == DT - 1 else None)
                    if kind == "dtr":
                        P.wait("scalar", "pe", v)
                        P.op("scalar", lambda e, mi=mi, c=c, b=b: e.activation(
                            out=dtr3[:, mi, c * TC:(c + 1) * TC],
                            in_=bank(b, cols=TC), func=AF.Copy), inc="act")
                        G.consumed(f"bank{b}", "act")
                    else:
                        row0 = 0 if kind == "B" else 32
                        P.wait("vector", "pe", v)
                        P.op("vector", lambda e, row0=row0, c=c, b=b: e.tensor_copy(
                            BC_sb[row0:row0 + N, c * TC:(c + 1) * TC],
                            bank(b, rows=N, cols=TC)), inc="dve")
                        G.consumed(f"bank{b}", "dve")
                first_mchunk[0] = False
            bc_ex_done = P.now("dve")

            bcn_done = P.now("act")
            ckpt("p3")
            P.flush(nc)
            abctx.close()

            cctx = ExitStack()
            csbt = lambda name, shape, dt=F32: cctx.enter_context(
                nc.sbuf_tensor(name, shape, dt))
            B_bc = [csbt(f"Bbc{n}", [128, T], BF16) for n in range(N)]
            C_bc = [csbt(f"Cbc{n}", [128, TO], BF16) for n in range(N)]
            xln_t = [csbt(f"xlnt{i}", [128, T], BF16) for i in range(2)]
            sq2_t = csbt("sq2_t", [128, TO], BF16)
            srow2 = csbt("srow2", [1, 2 * TO + 2 * 512 + 8])
            delta_t = [csbt(f"delta_t{i}", [128, T], BF16) for i in range(2)]
            u_t = [csbt(f"u_t{i}", [128, T], BF16) for i in range(2)]
            dA_b = [csbt(f"dAb{i}", [128, T]) for i in range(2)]
            dbu_t = csbt("dbu_t", [128, T], BF16)
            h_t = csbt("h_t", [128, T], BF16)
            y_t = csbt("y_t", [128, TO], BF16)
            xo_t = csbt("xo_t", [128, TO])
            x2_t = csbt("x2_t", [128, TO])
            x2bf_t = csbt("x2bf_t", [128, TO], BF16)
            sp_t = [csbt(f"sp_t{i}", [128, TC]) for i in range(2)]
            wsc_b = [csbt(f"wscb{i}", [128, PCW]) for i in range(3)]
            ty_t = dbu_t[:, 0:TO]
            P.wait("scalar", "s_xlw", xlw_all)
            P.wait("vector", "s_xlw", xlw_all)

            P.wait("gpsimd", "dve", bc_ex_done)
            for j in range(2 * N):
                pr = j % 2
                srcrow = j if j < N else 32 + (j - N)
                G.use(f"bcr{pr}", "gpsimd")
                v = P.op("gpsimd", lambda e, pr=pr, srcrow=srcrow: e.dma_start(
                    out=bc_rows[0:1, pr * T:(pr + 1) * T],
                    in_=BC_sb[srcrow:srcrow + 1, :]), inc=(f"s_br{pr}", 16))
                P.wait("tensor", f"s_br{pr}", v)
                if j < N:
                    k1_bcast(B_bc[j], bc_rows[0:1, pr * T:(pr + 1) * T], NC4)
                else:
                    k1_bcast(C_bc[j - N],
                             bc_rows[0:1, pr * T + O0:pr * T + O0 + TO], NC2)
                G.consumed(f"bcr{pr}", "pe")
            bcn_done = P.now("act")

            sc_state = {"ld": [], "p1": 0, "p2": 0}

            def emit_scale_load(which, p):
                buf = (p if which == 1 else NPC1 + p) % 3
                src = d_w1 if which == 1 else d_w2
                G.use(f"wsc{buf}", "gpsimd")
                v = P.op("gpsimd", lambda e, p=p, buf=buf, src=src: e.dma_start(
                    out=wsc_b[buf][:], in_=src[:, p * PCW:(p + 1) * PCW]),
                    inc=(f"s_wsc{buf}", 16))
                sc_state["ld"].append((which, p, buf, v))

            def emit_scale_compute():
                if not sc_state["ld"]:
                    return False
                which, p, buf, v = sc_state["ld"].pop(0)
                P.wait("scalar", f"s_wsc{buf}", v)
                dst = scpart1 if which == 1 else scpart2
                P.op("scalar", lambda e, buf=buf, p=p, dst=dst:
                     e.activation(
                         out=wsc_b[buf][:], in_=wsc_b[buf][:], func=AF.Abs,
                         accum_out=dst[:, p:p + 1]), inc="act")
                ab_v = P.now("act")
                G.consumed(f"wsc{buf}", "act", ab_v)
                if which == 1:
                    sc_state["p1"] = p + 1
                    if p + 1 == NPC1:
                        sc_state["w1_act"] = ab_v
                else:
                    sc_state["p2"] = p + 1
                    if p + 1 == NPC2:
                        sc_state["w2_act"] = ab_v
                return True

            sc_next = [0, 0]
            def queue_next_load():
                if sc_next[0] < NPC1:
                    emit_scale_load(1, sc_next[0])
                    sc_next[0] += 1
                elif sc_next[1] < NPC2:
                    emit_scale_load(2, sc_next[1])
                    sc_next[1] += 1
            for _ in range(3):
                queue_next_load()

            def pump_scale(nsteps):
                for _ in range(nsteps):
                    if not emit_scale_compute():
                        return
                    queue_next_load()

            tsc = srow2[0:1, 2 * TO + 2 * 512:2 * TO + 2 * 512 + 2]
            mrow = srow2[0:1, 2 * TO + 2 * 512 + 2:2 * TO + 2 * 512 + 8]
            thr_ready = {}

            def emit_thr(j):
                P.vforce = True
                part = scpart1 if j == 0 else scpart2
                acc_done = sc_state["w1_act"] if j == 0 else sc_state["w2_act"]
                sl = srow2[0:1, 2 * TO + j * 128:2 * TO + (j + 1) * 128]
                sl2 = srow2[0:1, 2 * TO + 256 + j * 128:2 * TO + 256 + (j + 1) * 128]
                P.wait("vector", "act", acc_done)
                P.op("vector", lambda e: e.tensor_reduce(
                    out=spcol[:, j:j + 1], in_=part[:], axis=mybir.AxisListType.X,
                    op=ALU.add), inc="dve")
                spc_v = P.now("dve")
                P.wait("sync", "dve", spc_v)
                P.op("sync", lambda e: e.dma_start(
                    out=d_scp[:, j:j + 1], in_=spcol[:, j:j + 1],
                    allow_slow_non_contiguous=True), inc=("s_thr", 16))
                P.wait("sync", "s_thr", P.now("s_thr"))
                P.op("sync", lambda e: e.dma_start(
                    out=sl,
                    in_=d_scp[:, j:j + 1].rearrange("p n -> (p n)")[None, :],
                    allow_slow_non_contiguous=True), inc=("s_thr", 16))
                P.wait("vector", "s_thr", P.now("s_thr"))
                P.op("vector", lambda e: e.tensor_reduce(
                    out=mrow[:, j:j + 1], in_=sl, axis=mybir.AxisListType.X,
                    op=ALU.add))
                P.op("vector", lambda e: e.tensor_scalar(
                    out=mrow[:, j:j + 1], in0=mrow[:, j:j + 1],
                    scalar1=1.0 / 128, scalar2=None, op0=ALU.mult))
                P.op("vector", lambda e: e.tensor_scalar(
                    out=sl2, in0=sl, scalar1=mrow[:, j:j + 1], scalar2=None,
                    op0=ALU.subtract))
                P.op("vector", lambda e: e.tensor_reduce(
                    out=mrow[:, 2 + j:3 + j], in_=sl2, axis=mybir.AxisListType.X,
                    op=ALU.add))
                P.op("vector", lambda e: e.tensor_scalar(
                    out=tsc[:, j:j + 1], in0=mrow[:, j:j + 1], scalar1=128.0,
                    scalar2=mrow[:, 2 + j:3 + j], op0=ALU.mult, op1=ALU.add))
                P.op("vector", lambda e: e.tensor_scalar(
                    out=tsc[:, j:j + 1], in0=tsc[:, j:j + 1], scalar1=1.0 / NW,
                    scalar2=cfg["W_EPS"], op0=ALU.mult, op1=ALU.max))
                P.op("vector", lambda e: e.tensor_scalar(
                    out=tsc[:, j:j + 1], in0=tsc[:, j:j + 1], scalar1=0.5,
                    scalar2=None, op0=ALU.mult), inc="dve")
                thr_v = P.now("dve")
                P.wait("sync", "dve", thr_v)
                P.op("sync", lambda e: e.dma_start(
                    out=d_thr[0:1, j:j + 1], in_=tsc[:, j:j + 1],
                    allow_slow_non_contiguous=True), inc=("s_thr", 16))
                P.wait("sync", "s_thr", P.now("s_thr"))
                P.op("sync", lambda e: e.dma_start(
                    out=thr_cols[:, 2 * j:2 * j + 1],
                    in_=d_thr[0:1, j:j + 1].partition_broadcast(128)),
                    inc=("s_thr", 16))
                tb = P.now("s_thr")
                P.wait("vector", "s_thr", tb)
                P.op("vector", lambda e: e.tensor_scalar(
                    out=thr_cols[:, 2 * j + 1:2 * j + 2],
                    in0=thr_cols[:, 2 * j:2 * j + 1], scalar1=-1.0,
                    scalar2=None, op0=ALU.mult), inc="dve")
                thr_ready[j] = P.now("dve")
                P.vforce = False

            pump_per_dt = (NPC1 + NPC2 + DT - 1) // DT

            def pump_point(state):
                if state["budget"] > 0:
                    pump_scale(1)
                    state["budget"] -= 1
                if "w1_act" in sc_state and 0 not in thr_ready:
                    emit_thr(0)
            P.wait("sync", "s_xlw", xlw_all)
            xl_rd = {}
            def emit_xln_read(dt):
                G.use(f"xlt{dt % 2}", "sync")
                xl_rd[dt] = P.op("sync", lambda e, dt=dt: e.dma_start(
                    out=xln_t[dt % 2][:], in_=d_xln[dt * 128:(dt + 1) * 128, :]),
                    inc=(f"s_xl{dt % 2}", 16))
            for dt in range(min(2, DT)):
                emit_xln_read(dt)

            delta_done = {}
            def emit_delta_chunk(dt, c):
                dp = dt % 2
                b = c % 2
                G.use(f"bank{b}", "tensor")
                v = None
                for k in range(RK):
                    v = P.op("tensor", lambda e, k=k, c=c, dt=dt, b=b:
                             e.matmul(bank(b, cols=TC),
                                      wdt3[:, k, dt * 128:(dt + 1) * 128],
                                      dtr3[:, k, c * TC:(c + 1) * TC],
                                      start=(k == 0), stop=(k == RK - 1)),
                             inc="pe" if k == RK - 1 else None)
                P.wait("scalar", "pe", v)
                if c == 0:
                    G.use(f"delta{dp}", "scalar")
                P.op("scalar", lambda e, c=c, dt=dt, b=b, dp=dp: e.activation(
                    out=delta_t[dp][:, c * TC:(c + 1) * TC],
                    in_=bank(b, cols=TC),
                    func=AF.Relu, bias=bdt_c[:, dt:dt + 1], scale=1.0))
                G.use(f"sp{c % 2}", "scalar")
                P.op("scalar", lambda e, c=c, dt=dt, b=b: e.activation(
                    out=sp_t[c % 2][:], in_=bank(b, cols=TC), func=AF.Abs,
                    bias=bdt_c[:, dt:dt + 1], scale=1.0), inc="act")
                G.consumed(f"bank{b}", "act")
                P.op("scalar", lambda e, c=c: e.activation(
                    out=sp_t[c % 2][:], in_=sp_t[c % 2][:], func=AF.Exp,
                    scale=-1.0))
                P.op("scalar", lambda e, c=c: e.activation(
                    out=sp_t[c % 2][:], in_=sp_t[c % 2][:], func=AF.Ln,
                    bias=cvals[:, 1:2], scale=1.0), inc="act")
                spl_v = P.now("act")
                P.wait("vector", "act", spl_v)
                P.op("vector", lambda e, c=c, dp=dp: e.tensor_tensor(
                    out=delta_t[dp][:, c * TC:(c + 1) * TC],
                    in0=delta_t[dp][:, c * TC:(c + 1) * TC], in1=sp_t[c % 2][:],
                    op=ALU.add), inc="dve")
                G.consumed(f"sp{c % 2}", "dve")
                if c == NC4 - 1:
                    delta_done[dt] = P.now("dve")

            def emit_u(dt):
                dp = dt % 2
                P.wait("vector", f"s_xl{dp}", xl_rd[dt])
                G.use(f"u{dp}", "vector")
                P.op("vector", lambda e, dp=dp: e.tensor_tensor(
                    out=u_t[dp][:], in0=delta_t[dp][:], in1=xln_t[dp][:],
                    op=ALU.mult), inc="dve")

            voff("p4")
            for c in range(NC4):
                emit_delta_chunk(0, c)
            emit_u(0)
            for dt in range(DT):
                dp = dt % 2
                pstate = {"budget": pump_per_dt}
                G.use("xo_t", "sync")
                xo_v = P.op("sync", lambda e, dt=dt: e.dma_start(
                    out=xo_t[:], in_=d_x[dt * 128:(dt + 1) * 128, O0:T]),
                    inc=("s_xo", 16))
                if dt == 0:
                    ckpt("p4a")
                for n in range(N):
                    nb = n % 2
                    G.use(f"dA{nb}", "scalar")
                    P.wait("scalar", "dve", delta_done[dt])
                    av = P.op("scalar", lambda e, n=n, dt=dt, nb=nb, dp=dp:
                              e.activation(
                                  out=dA_b[nb][:], in_=delta_t[dp][:],
                                  func=AF.Exp,
                                  scale=negA[:, dt * N + n:dt * N + n + 1]),
                              inc="act")
                    P.op("vector", lambda e, n=n, dp=dp: e.tensor_tensor(
                        out=dbu_t[:], in0=u_t[dp][:], in1=B_bc[n][:],
                        op=ALU.mult))
                    P.wait("vector", "act", av)
                    P.vforce = True
                    P.op("vector", lambda e, nb=nb: e.tensor_tensor_scan(
                        out=h_t[:], data0=dA_b[nb][:], data1=dbu_t[:],
                        initial=0.0, op0=ALU.mult, op1=ALU.add), inc="dve")
                    P.vforce = False
                    G.consumed(f"dA{nb}", "dve")
                    if n == 0:
                        P.op("vector", lambda e, n=n: e.tensor_tensor(
                            out=y_t[:], in0=h_t[:, O0:T], in1=C_bc[n][:],
                            op=ALU.mult))
                    else:
                        P.op("vector", lambda e, n=n: e.tensor_tensor(
                            out=ty_t, in0=h_t[:, O0:T], in1=C_bc[n][:],
                            op=ALU.mult))
                        P.op("vector", lambda e: e.tensor_tensor(
                            out=y_t[:], in0=y_t[:], in1=ty_t, op=ALU.add))
                    if dt + 1 < DT and n < NC4:
                        emit_delta_chunk(dt + 1, n)
                    if dt + 1 < DT and n == NC4:
                        emit_u(dt + 1)
                    if n in (0, 2, 4, 6):
                        pump_point(pstate)
                G.consumed(f"delta{dp}", "act")
                G.consumed(f"u{dp}", "dve")
                if dt == 0:
                    ckpt("p4b")
                P.op("vector", lambda e, dt=dt, dp=dp: e.scalar_tensor_tensor(
                    out=y_t[:], in0=xln_t[dp][:, O0:T],
                    scalar=dpar_c[:, dt:dt + 1], in1=y_t[:],
                    op0=ALU.mult, op1=ALU.add))
                G.consumed(f"xlt{dp}", "dve")
                if dt + 2 < DT:
                    emit_xln_read(dt + 2)
                P.wait("vector", "s_xo", xo_v)
                G.use("x2_t", "vector")
                P.op("vector", lambda e: e.tensor_tensor(
                    out=x2_t[:], in0=xo_t[:], in1=y_t[:], op=ALU.add),
                    inc="dve")
                x2_v = P.now("dve")
                G.consumed("xo_t", "dve")
                P.wait("sync", "dve", x2_v)
                P.op("sync", lambda e, dt=dt: e.dma_start(
                    out=d_x2[dt * 128:(dt + 1) * 128, :], in_=x2_t[:]),
                    inc=("s_x2w", 16))
                G.consumed("x2_t", "s_x2w")
                G.use("x2bf_t", "vector")
                P.op("vector", lambda e: e.tensor_copy(x2bf_t[:], x2_t[:]))
                G.use("sq2", "vector")
                P.op("vector", lambda e: e.tensor_tensor(
                    out=sq2_t[:], in0=x2bf_t[:], in1=x2bf_t[:], op=ALU.mult),
                    inc="dve")
                s2_done = P.now("dve")
                P.wait("tensor", "dve", s2_done)
                if dt == 0:
                    for b in range(4, 8):
                        G.use(f"bank{b}", "tensor")
                for c in range(NC2):
                    P.op("tensor", lambda e, c=c, dt=dt: e.matmul(
                        bank(4 + c, rows=1, cols=TC), ones_cb[:],
                        x2bf_t[:, c * TC:(c + 1) * TC],
                        start=(dt == 0), stop=(dt == DT - 1)))
                for c in range(NC2):
                    P.op("tensor", lambda e, c=c, dt=dt: e.matmul(
                        bank(4 + NC2 + c, rows=1, cols=TC), ones_cb[:],
                        sq2_t[:, c * TC:(c + 1) * TC],
                        start=(dt == 0), stop=(dt == DT - 1)),
                        inc="pe" if c == NC2 - 1 else None)
                G.consumed("sq2", "pe")
                G.consumed("x2bf_t", "pe")
                if dt == 0:
                    ckpt("p4c")
                while pstate["budget"] > 0:
                    pump_point(pstate)
                if dt == 0:
                    ckpt("p4d")
            stats2_done = P.now("pe")
            von()
            ckpt("p4")
            pump_scale(NPC1 + NPC2)
            if 0 not in thr_ready:
                emit_thr(0)
            emit_thr(1)

            ckpt("thr", dump=(thr_cols[:], 128, 4))

            voff("ln2rows")
            s2x = srow2[0:1, 0:TO]
            s2q = srow2[0:1, TO:2 * TO]
            tmp2 = srow_bf[0:1, TO:2 * TO]
            P.wait("vector", "pe", stats2_done)
            for c in range(NC2):
                P.op("vector", lambda e, c=c: e.tensor_copy(
                    s2x[:, c * TC:(c + 1) * TC], bank(4 + c, rows=1, cols=TC)))
            for c in range(NC2):
                P.op("vector", lambda e, c=c: e.tensor_copy(
                    s2q[:, c * TC:(c + 1) * TC],
                    bank(4 + NC2 + c, rows=1, cols=TC)),
                    inc=("dve", 1) if c == NC2 - 1 else None)
            stats2_ev = P.now("dve")
            for b in range(4, 8):
                G.consumed(f"bank{b}", "dve", stats2_ev)
            P.op("vector", lambda e: e.tensor_scalar(
                out=s2x, in0=s2x, scalar1=1.0 / D, scalar2=None, op0=ALU.mult))
            P.op("vector", lambda e: e.tensor_scalar(
                out=s2q, in0=s2q, scalar1=1.0 / D, scalar2=None, op0=ALU.mult))
            P.op("vector", lambda e: e.tensor_tensor(
                out=tmp2, in0=s2x, in1=s2x, op=ALU.mult))
            P.op("vector", lambda e: e.tensor_tensor(
                out=s2q, in0=s2q, in1=tmp2, op=ALU.subtract), inc="dve")
            var2_done = P.now("dve")
            P.wait("scalar", "dve", var2_done)
            P.op("scalar", lambda e: e.activation(
                out=s2q, in_=s2q, func=AF.Ln, bias=cvals[0:1, 0:1], scale=1.0))
            P.op("scalar", lambda e: e.activation(
                out=s2q, in_=s2q, func=AF.Exp, scale=-0.5), inc="act")
            r2_done = P.now("act")
            P.wait("vector", "act", r2_done)
            P.op("vector", lambda e: e.tensor_tensor(
                out=s2x, in0=s2x, in1=s2q, op=ALU.mult))
            P.op("vector", lambda e: e.tensor_copy(srow_bf[0:1, 0:TO], s2q))
            P.op("vector", lambda e: e.tensor_copy(srow_bf[0:1, T:T + TO], s2x),
                 inc="dve")
            rows2_bf = P.now("dve")
            von()
            P.flush(nc)
            cctx.close()

        bctx = ExitStack()
        with bctx:
            def bsbt(name, shape, dt=F32):
                return bctx.enter_context(nc.sbuf_tensor(name, shape, dt))
            h_sb = bsbt("h_sb", [128, FT * TO], BF16)
            f1ctx = ExitStack()
            f1sbt = lambda name, shape, dt=F32: f1ctx.enter_context(
                nc.sbuf_tensor(name, shape, dt))
            xln2 = [f1sbt(f"xln2_{i}", [128, TO], BF16) for i in range(DT)]
            r2_bc = f1sbt("r2_bc", [128, TO], BF16)
            m2_bc = f1sbt("m2_bc", [128, TO], BF16)
            x2r_t = [f1sbt(f"x2rt{i}", [128, TO]) for i in range(2)]
            w1s_b = [f1sbt(f"w1sb{i}", [128, DT * 128]) for i in range(2)]
            w1q_b = [f1sbt(f"w1qb{i}", [128, DT * 128], BF16) for i in range(2)]

            w1u_v = {}
            SLW1 = DT * 128
            def emit_w1_load(ft):
                G.use(f"w1s{ft % 2}", "sync")
                w1u_v[ft] = P.op("sync", lambda e, ft=ft: e.dma_start(
                    out=w1s_b[ft % 2][:],
                    in_=d_w1[:, ft * SLW1:(ft + 1) * SLW1]),
                    inc=(f"s_w1{ft % 2}", 16))
            tern1_v = {}
            def emit_w1_tern(ft):
                voff("ffn1")
                fb = ft % 2
                P.wait("vector", f"s_w1{fb}", w1u_v[ft])
                G.use(f"w1q{fb}", "vector")
                P.op("vector", lambda e, fb=fb: e.tensor_scalar(
                    out=w1q_b[fb][:], in0=w1s_b[fb][:], scalar1=thr_cols[:, 1:2],
                    scalar2=None, op0=ALU.is_lt))
                P.op("vector", lambda e, fb=fb: e.scalar_tensor_tensor(
                    out=w1q_b[fb][:], in0=w1s_b[fb][:], scalar=thr_cols[:, 0:1],
                    in1=w1q_b[fb][:], op0=ALU.is_gt, op1=ALU.subtract), inc="dve")
                tern1_v[ft] = P.now("dve")
                von()
                G.consumed(f"w1s{fb}", "dve")
            for ft in range(min(2, FT)):
                emit_w1_load(ft)
            P.wait("vector", "dve", thr_ready[0])
            for ft in range(min(2, FT)):
                emit_w1_tern(ft)

            P.wait("tensor", "dve", rows2_bf)
            k1_bcast(r2_bc, srow_bf[0:1, 0:TO], NC2, bank_pair=(0, 1))
            k1_bcast(m2_bc, srow_bf[0:1, T:T + TO], NC2, bank_pair=(0, 1))
            bc2_done = P.now("act")

            P.wait("vector", "act", bc2_done)
            voff("xln2")
            x2r_v = {}
            def emit_x2r(i):
                G.use(f"x2r{i % 2}", "sync")
                P.wait("sync", "s_x2w", 16 * (i + 1))
                x2r_v[i] = P.op("sync", lambda e, i=i: e.dma_start(
                    out=x2r_t[i % 2][:], in_=d_x2[i * 128:(i + 1) * 128, :]),
                    inc=(f"s_x2r{i % 2}", 16))
            for i in range(min(2, DT)):
                emit_x2r(i)
            xln2_v = {}
            for i in range(DT):
                P.wait("vector", f"s_x2r{i % 2}", x2r_v[i])
                P.op("vector", lambda e, i=i: e.tensor_tensor(
                    out=xln2[i][:], in0=x2r_t[i % 2][:], in1=r2_bc[:],
                    op=ALU.mult))
                P.op("vector", lambda e, i=i: e.tensor_tensor(
                    out=xln2[i][:], in0=xln2[i][:], in1=m2_bc[:], op=ALU.subtract),
                    inc="dve")
                xln2_v[i] = P.now("dve")
                G.consumed(f"x2r{i % 2}", "dve")
                if i + 2 < DT:
                    emit_x2r(i + 2)
            xln2_done = P.now("dve")
            von()
            ckpt("xln2")

            for ft in range(FT):
                fb = ft % 2
                P.wait("tensor", "dve", tern1_v[ft])
                if ft == 0:
                    P.wait("tensor", "act", bc2_done)
                for c in range(NC2):
                    b = (ft * NC2 + c) % 2
                    G.use(f"bank{b}", "tensor")
                    v = None
                    for k in range(DT):
                        if ft == 0:
                            P.wait("tensor", "dve", xln2_v[k])
                        v = P.op("tensor", lambda e, k=k, c=c, fb=fb, b=b:
                                 e.matmul(bank(b, cols=TC),
                                          w1q_b[fb][:, k * 128:(k + 1) * 128],
                                          xln2[k][:, c * TC:(c + 1) * TC],
                                          start=(k == 0), stop=(k == DT - 1)),
                                 inc="pe" if k == DT - 1 else None)
                    P.wait("scalar", "pe", v)
                    P.op("scalar", lambda e, ft=ft, c=c, b=b: e.activation(
                        out=h_sb[:, ft * TO + c * TC: ft * TO + (c + 1) * TC],
                        in_=bank(b, cols=TC), func=AF.Gelu,
                        bias=b1_c[:, ft:ft + 1], scale=1.0), inc="act")
                    G.consumed(f"bank{b}", "act")
                G.consumed(f"w1q{fb}", "pe")
                if ft + 2 < FT:
                    emit_w1_load(ft + 2)
                    emit_w1_tern(ft + 2)
            ffn1_done = P.now("act")
            ckpt("ffn1")
            P.flush(nc)
            f1ctx.close()

            f2ctx = bctx.enter_context(ExitStack())
            f2sbt = lambda name, shape, dt=F32: f2ctx.enter_context(
                nc.sbuf_tensor(name, shape, dt))
            w2s_b = [f2sbt(f"w2sb{i}", [128, KSK * DQW]) for i in range(4)]
            w2q_b = [f2sbt(f"w2qb{i}", [128, KSK * DQW], BF16) for i in range(2)]
            oev_b = [f2sbt(f"oevb{i}", [128, TC]) for i in range(2)]
            x2e_b = [f2sbt(f"x2eb{i}", [128, TC]) for i in range(4)]
            SLW2 = KSK * DQW
            w2u_v = {}
            def emit_w2_load(idx):
                q = "sync" if idx % 2 == 0 else "scalar"
                G.use(f"w2s{idx % 4}", q)
                w2u_v[idx] = P.op(q, lambda e, idx=idx: e.dma_start(
                    out=w2s_b[idx % 4][:],
                    in_=d_w2[:, idx * SLW2:(idx + 1) * SLW2]),
                    inc=(f"s_w2s{idx % 4}", 16), dma=True)
            for idx in range(min(4, DQ * NKS)):
                emit_w2_load(idx)
            P.wait("tensor", "act", ffn1_done)
            P.wait("vector", "dve", thr_ready[1])
            voff("ffn2")

            x2e_v = {}
            def emit_x2e_loads(dq):
                for j in range(DQT * NC2):
                    ck = dq * DQT * NC2 + j
                    gdt = dq * DQT + j // NC2
                    c = j % NC2
                    eb = ck % 4
                    G.use(f"x2eb{eb}", "sync")
                    x2e_v[ck] = P.op("sync", lambda e, gdt=gdt, c=c, eb=eb:
                                     e.dma_start(
                                         out=x2e_b[eb][:],
                                         in_=d_x2[gdt * 128:(gdt + 1) * 128,
                                                  c * TC:(c + 1) * TC]),
                                     inc=(f"s_x2e{eb}", 16))
            evst = {"chunk": 0}
            def emit_evacs(dq, mmv):
                base = (dq % 2) * 4 if DQ > 1 else 0
                for dtq in range(DQT):
                    gdt = dq * DQT + dtq
                    for c in range(NC2):
                        ck = evst["chunk"]
                        eb = ck % 4
                        evst["chunk"] += 1
                        ob = ck % 2
                        P.wait("vector", "pe", mmv)
                        P.wait("vector", f"s_x2e{eb}", x2e_v[ck])
                        G.use(f"oev{ob}", "vector")
                        bnum = base + dtq * NC2 + c
                        P.op("vector", lambda e, gdt=gdt, bnum=bnum, eb=eb,
                             ob=ob:
                             e.scalar_tensor_tensor(
                                 out=oev_b[ob][:],
                                 in0=ps[0:128, bnum * 512:bnum * 512 + TC],
                                 scalar=b2_c[:, gdt:gdt + 1], in1=x2e_b[eb][:],
                                 op0=ALU.add, op1=ALU.add), inc="dve")
                        ev = P.now("dve")
                        G.consumed(f"x2eb{eb}", "dve", ev)
                        G.consumed(f"bank{bnum}", "dve", ev)
                        P.wait("sync", "dve", ev)
                        P.op("sync", lambda e, gdt=gdt, c=c, ob=ob: e.dma_start(
                            out=d_out[gdt * 128:(gdt + 1) * 128,
                                      c * TC:(c + 1) * TC],
                            in_=oev_b[ob][:]), inc=(f"s_out{ob}", 16))
                        G.consumed(f"oev{ob}", f"s_out{ob}")

            emit_x2e_loads(0)
            pending = None
            for dq in range(DQ):
                base = (dq % 2) * 4 if DQ > 1 else 0
                for b in range(base, base + DQT * NC2):
                    G.use(f"bank{b}", "tensor")
                for ks in range(NKS):
                    idx = dq * NKS + ks
                    sb4 = idx % 4
                    sb2 = idx % 2
                    P.wait("vector", f"s_w2s{sb4}", w2u_v[idx])
                    G.use(f"w2q{sb2}", "vector")
                    P.op("vector", lambda e, sb2=sb2, sb4=sb4: e.tensor_scalar(
                        out=w2q_b[sb2][:], in0=w2s_b[sb4][:],
                        scalar1=thr_cols[:, 3:4],
                        scalar2=None, op0=ALU.is_lt))
                    P.op("vector", lambda e, sb2=sb2, sb4=sb4:
                         e.scalar_tensor_tensor(
                             out=w2q_b[sb2][:], in0=w2s_b[sb4][:],
                             scalar=thr_cols[:, 2:3], in1=w2q_b[sb2][:],
                             op0=ALU.is_gt, op1=ALU.subtract), inc="dve")
                    tern2_v = P.now("dve")
                    G.consumed(f"w2s{sb4}", "dve")
                    if ks == min(1, NKS - 1) and pending is not None:
                        emit_evacs(*pending)
                        pending = None
                        emit_x2e_loads(dq)
                    P.wait("tensor", "dve", tern2_v)
                    for kk in range(KSK):
                        kglob = ks * KSK + kk
                        for dtq in range(DQT):
                            for c in range(NC2):
                                bnum = base + dtq * NC2 + c
                                lastk = (kk == KSK - 1 and dtq == DQT - 1
                                         and c == NC2 - 1)
                                P.op("tensor", lambda e, kk=kk, dtq=dtq, c=c,
                                     sb2=sb2, kglob=kglob, bnum=bnum, ks=ks:
                                     e.matmul(
                                         ps[0:128, bnum * 512:bnum * 512 + TC],
                                         w2q_b[sb2][:, kk * DQW + dtq * 128:
                                                    kk * DQW + (dtq + 1) * 128],
                                         h_sb[:, kglob * TO + c * TC:
                                              kglob * TO + (c + 1) * TC],
                                         start=(ks == 0 and kk == 0),
                                         stop=(ks == NKS - 1 and kk == KSK - 1)),
                                     inc="pe" if lastk else None)
                    G.consumed(f"w2q{sb2}", "pe")
                    if idx + 4 < DQ * NKS:
                        emit_w2_load(idx + 4)
                mm_done = P.now("pe")
                pending = (dq, mm_done)
            emit_evacs(*pending)
            von()
            P.wait("gpsimd", "s_out0", P.now("s_out0"))
            P.wait("gpsimd", "s_out1", P.now("s_out1"))
            P.flush(nc)
    except _StopBuild:
        pass
    return nc


_CACHE = {}

def _build_full():
    if "nc" not in _CACHE:
        _CACHE["nc"] = build_nc(make_cfg(full=True))
    return _CACHE["nc"]


def kernel(L0_attn_x, L0_attn_W_xproj, L0_attn_W_dt, L0_attn_b_dt, L0_attn_A_log,
           L0_attn_D_param, L0_ffn_w1, L0_ffn_b1, L0_ffn_w2, L0_ffn_b2):
    x = np.asarray(L0_attn_x, np.float32)
    Wxp = np.asarray(L0_attn_W_xproj, np.float32)
    Wdt = np.asarray(L0_attn_W_dt, np.float32)
    bdt = np.asarray(L0_attn_b_dt, np.float32)
    Alog = np.asarray(L0_attn_A_log, np.float32)
    Dpar = np.asarray(L0_attn_D_param, np.float32)
    w1 = np.asarray(L0_ffn_w1, np.float32)
    b1 = np.asarray(L0_ffn_b1, np.float32)
    w2 = np.asarray(L0_ffn_w2, np.float32)
    b2 = np.asarray(L0_ffn_b2, np.float32)

    B, L, D = x.shape
    TO = L // 2
    N = Alog.shape[1]
    DT = D // 128
    FT = w1.shape[0] // 128

    wxpT = np.ascontiguousarray(Wxp.T)
    wdtT = np.ascontiguousarray(Wdt.T)
    Fdim = w1.shape[0]
    DQW = min(256, D)
    w1T = np.ascontiguousarray(
        w1.T.reshape(DT, 128, FT, 128).transpose(1, 2, 0, 3).reshape(128, -1))
    w2T = np.ascontiguousarray(
        w2.T.reshape(Fdim // 128, 128, D // DQW, DQW)
        .transpose(1, 2, 0, 3).reshape(128, -1))
    bdt_l = np.ascontiguousarray(bdt.reshape(DT, 128).T)
    dpar_l = np.ascontiguousarray(Dpar.reshape(DT, 128).T)
    alog_l = np.ascontiguousarray(
        Alog.reshape(DT, 128, N).transpose(1, 0, 2).reshape(128, DT * N))
    b1_l = np.ascontiguousarray(b1.reshape(FT, 128).T)
    b2_l = np.ascontiguousarray(b2.reshape(DT, 128).T)

    in_maps = []
    for c in range(8):
        b, r = c // 2, c % 2
        xT = np.ascontiguousarray(x[b].T)
        own = xT[:, r * TO:(r + 1) * TO]
        prefix = np.zeros((D, TO), np.float32) if r == 0 else xT[:, 0:TO]
        x_c = np.ascontiguousarray(np.concatenate([prefix, own], axis=1))
        in_maps.append({
            "x": x_c, "wxp": wxpT, "wdt": wdtT, "bdt": bdt_l, "alog": alog_l,
            "dpar": dpar_l, "w1t": w1T, "b1": b1_l, "w2t": w2T, "b2": b2_l,
        })

    nc = _build_full()
    res = run_bass_kernel_spmd(nc, in_maps, core_ids=list(range(8)))

    out = np.empty((B, L, D), np.float32)
    for c in range(8):
        b, r = c // 2, c % 2
        out[b, r * TO:(r + 1) * TO, :] = res.results[c]["out"].T
    return out


# version 19
# speedup vs baseline: 1.0370x; 1.0193x over previous
import numpy as np
import ml_dtypes
import concourse.bass as bass
import concourse.mybir as mybir
from concourse.bass_utils import run_bass_kernel_spmd
from contextlib import ExitStack

F32 = mybir.dt.float32
BF16 = mybir.dt.bfloat16
FP16 = mybir.dt.float16
AF = mybir.ActivationFunctionType
ALU = mybir.AluOpType

ENGS = ("sync", "gpsimd", "vector", "scalar", "tensor")


def make_cfg(full=True):
    if full:
        D, T, TO, R, F, N, TC = 2048, 2048, 1024, 256, 8192, 8, 512
        SCP = 128
    else:
        D, T, TO, R, F, N, TC = 256, 256, 128, 128, 512, 8, 64
        SCP = 128
    cfg = dict(D=D, T=T, TO=TO, R=R, F=F, N=N, TC=TC, SCP=SCP)
    cfg["DT"] = D // 128
    cfg["FT"] = F // 128
    cfg["RK"] = R // 128
    cfg["NC4"] = T // TC
    cfg["NC2"] = TO // TC
    cfg["O0"] = T - TO
    cfg["DQW"] = min(256, D)
    cfg["DQ"] = D // cfg["DQW"]
    cfg["KS"] = min(512, F)
    cfg["LN_EPS"] = 1e-5
    cfg["W_EPS"] = 1e-8
    return cfg


class Prog:
    ESEM = {"vector": "dve", "scalar": "act", "gpsimd": "pool"}

    def __init__(self, paranoid=True, self_waits=False):
        self.ops = {e: [] for e in ENGS}
        self.cnt = {}
        self.sems = {}
        self.paranoid = paranoid
        if self_waits is True:
            self_waits = {"vector", "scalar", "gpsimd"}
        self.self_waits = set(self_waits) if self_waits else set()
        self.vforce = False
        self.voff = False
        self.vrelax = set()

    def sem(self, name):
        self.cnt.setdefault(name, 0)
        return name

    def op(self, eng, fn, inc=None, dma=False):
        esem = self.ESEM.get(eng)
        incs = []
        if inc is not None:
            if isinstance(inc, str):
                inc = (inc, 1)
            incs.append(inc)
        if (self.paranoid and not dma and esem is not None
                and eng in ("vector", "scalar")):
            if not any(n == esem for n, _ in incs):
                incs.append((esem, 1))
            if ((eng in self.self_waits and not (eng == "vector" and self.voff))
                    or (eng == "vector" and self.vforce)):
                self.wait(eng, esem, self.now(esem))
        if incs:
            for name, by in incs:
                self.sem(name)
                self.cnt[name] += by
            sems = self.sems
            def wrapped(e, fn=fn, incs=tuple(incs)):
                ins = fn(e)
                for name, by in incs:
                    ins = ins.then_inc(sems[name], by)
            self.ops[eng].append(wrapped)
            return self.cnt[inc[0]] if inc is not None else self.cnt[esem]
        self.ops[eng].append(fn)
        return None

    def wait(self, eng, sem_name, val):
        if val is None or val <= 0:
            return
        sems = self.sems
        self.ops[eng].append(lambda e, s=sem_name, v=val: e.wait_ge(sems[s], v))

    def now(self, name):
        return self.cnt.get(name, 0)

    def flush(self, nc):
        pending = {e: ops for e, ops in self.ops.items() if ops}
        if not pending:
            return
        with nc.Block() as blk:
            for eng_name, ops in pending.items():
                getattr(blk, eng_name)(lambda e, ops=ops: [f(e) for f in ops])
        self.ops = {e: [] for e in ENGS}


class Guard:
    def __init__(self, prog):
        self.P = prog
        self.m = {}

    def consumed(self, key, sem, val=None):
        self.m.setdefault(key, {})[sem] = self.P.now(sem) if val is None else val

    def use(self, key, eng):
        for sem, val in self.m.pop(key, {}).items():
            self.P.wait(eng, sem, val)


class _StopBuild(Exception):
    pass


def build_nc(cfg, stop=None, self_waits=("vector",),
             vrelax=("p1cast", "ln1rows", "ln2rows", "xln2", "ffn1", "ffn2",
                     "p4", "p0", "p3apply")):
    D, T, TO, R, F, N, TC, SCP = (cfg[k] for k in
                                  ("D", "T", "TO", "R", "F", "N", "TC", "SCP"))
    DT, FT, RK, NC4, NC2, O0 = (cfg[k] for k in
                                ("DT", "FT", "RK", "NC4", "NC2", "O0"))
    DQ, DQW, KS = cfg["DQ"], cfg["DQW"], cfg["KS"]
    NW = D * F
    FK = F // 128
    DQT = DQW // 128
    KSK = KS // 128
    NKS = F // KS
    WTOT = D * F // 128
    PCW = min(2048, WTOT)
    NPC1 = WTOT // PCW
    NPC2 = WTOT // PCW

    nc = bass.Bass("TRN2", target_bir_lowering=False)
    d_x = nc.dram_tensor("x", [D, T], F32, kind="ExternalInput").ap()
    d_wxp = nc.dram_tensor("wxp", [D, R + 2 * N], F32, kind="ExternalInput").ap()
    d_wdt = nc.dram_tensor("wdt", [R, D], F32, kind="ExternalInput").ap()
    d_bdt = nc.dram_tensor("bdt", [128, DT], F32, kind="ExternalInput").ap()
    d_alog = nc.dram_tensor("alog", [128, DT * N], F32, kind="ExternalInput").ap()
    d_dpar = nc.dram_tensor("dpar", [128, DT], F32, kind="ExternalInput").ap()
    d_w1 = nc.dram_tensor("w1t", [128, WTOT], F32, kind="ExternalInput").ap()
    d_b1 = nc.dram_tensor("b1", [128, FT], F32, kind="ExternalInput").ap()
    d_w2 = nc.dram_tensor("w2t", [128, WTOT], F32, kind="ExternalInput").ap()
    d_b2 = nc.dram_tensor("b2", [128, DT], F32, kind="ExternalInput").ap()
    d_out = nc.dram_tensor("out", [D, TO], F32, kind="ExternalOutput").ap()
    d_x2 = nc.dram_tensor("x2s", [D, TO], F32, kind="Internal").ap()
    d_xln = nc.dram_tensor("xlns", [D, T], BF16, kind="Internal").ap()
    d_thr = nc.dram_tensor("thrs", [1, 2], F32, kind="Internal").ap()
    d_scp = nc.dram_tensor("scps", [128, 2], F32, kind="Internal").ap()

    P = Prog(self_waits=self_waits)
    P.vrelax = set(vrelax)
    G = Guard(P)

    def voff(name):
        P.voff = name in P.vrelax

    def von():
        P.voff = False

    def ckpt(tag, dump=None):
        if stop == tag:
            if dump is not None:
                P.wait("sync", "dve", P.now("dve"))
                P.op("sync", lambda e: e.dma_start(
                    out=d_out[0:dump[1], 0:dump[2]], in_=dump[0]),
                    inc=("s_out0", 16))
                P.wait("gpsimd", "s_out0", P.now("s_out0"))
            P.wait("gpsimd", "dve", P.now("dve"))
            P.wait("gpsimd", "act", P.now("act"))
            P.wait("gpsimd", "pe", P.now("pe"))
            P.flush(nc)
            raise _StopBuild

    ctx = ExitStack()
    try:
      with ctx:
        def sbt(name, shape, dt=F32):
            return ctx.enter_context(nc.sbuf_tensor(name, shape, dt))

        bdt_c = sbt("bdt_c", [128, DT])
        negA = sbt("negA", [128, DT * N])
        dpar_c = sbt("dpar_c", [128, DT])
        b1_c = sbt("b1_c", [128, FT])
        b2_c = sbt("b2_c", [128, DT])
        ones_cb = sbt("ones_cb", [128, 1], BF16)
        ones_rb = sbt("ones_rb", [1, 128], BF16)
        cvals = sbt("cvals", [128, 2])
        srow_bf = sbt("srow_bf", [1, 2 * T], BF16)
        thr_cols = sbt("thr_cols", [128, 4])

        ps = ctx.enter_context(nc.psum_tensor("ps", [128, 4096], F32))
        def bank(b, rows=128, cols=512, c0=0):
            return ps[0:rows, b * 512 + c0: b * 512 + c0 + cols]

        for s in ("s_x0", "s_x1", "s_x2", "s_x3", "s_sm", "s_smg", "s_smg2",
                  "s_wsc0", "s_wsc1", "s_wsc2", "s_w10", "s_w11",
                  "s_w2s0", "s_w2s1", "s_w2s2", "s_w2s3",
                  "s_xo", "s_x2w", "s_x2r0", "s_x2r1", "s_x2e0", "s_x2e1",
                  "s_x2e2", "s_x2e3",
                  "s_out0", "s_out1", "s_br0", "s_br1", "s_thr", "s_xlw", "s_xl0", "s_xl1",
                  "pe", "act", "dve", "pool"):
            P.sem(s)
            P.sems[s] = ctx.enter_context(nc.semaphore(s))

        for dst, src in ((bdt_c, d_bdt), (dpar_c, d_dpar), (b1_c, d_b1),
                         (b2_c, d_b2), (negA, d_alog)):
            P.op("sync", lambda e, dd=dst, ss=src: e.dma_start(out=dd[:], in_=ss),
                 inc=("s_sm", 16))
        sm_all = P.now("s_sm")
        P.wait("scalar", "s_sm", sm_all)
        P.op("scalar", lambda e: e.activation(out=negA[:], in_=negA[:],
                                              func=AF.Exp), inc="act")
        aexp_done = P.now("act")
        P.wait("vector", "act", aexp_done)
        voff("p0")
        P.op("vector", lambda e: e.tensor_scalar(
            out=negA[:], in0=negA[:], scalar1=-1.0, scalar2=None, op0=ALU.mult))
        P.op("vector", lambda e: e.memset(ones_cb[:], 1.0))
        P.op("vector", lambda e: e.memset(ones_rb[:], 1.0))
        P.op("vector", lambda e: e.memset(cvals[:, 0:1], 1e-5))
        P.op("vector", lambda e: e.memset(cvals[:, 1:2], 1.0), inc="dve")
        setup_done = P.now("dve")
        von()

        actx = ExitStack()
        with actx:
            def asbt(name, shape, dt=F32):
                return actx.enter_context(nc.sbuf_tensor(name, shape, dt))
            wdt_bf = asbt("wdt_bf", [128, RK * D], BF16)
            dtr_bf = asbt("dtr_bf", [128, RK * T], BF16)
            bc_rows = asbt("bc_rows", [1, 2 * T], BF16)
            BC_sb = asbt("BC_sb", [32 + N, T])
            ones_f16 = asbt("ones_f16", [128, 1], FP16)
            ones_cf = asbt("ones_cf", [128, 1])
            P.op("vector", lambda e: e.memset(ones_cf[:], 1.0))
            P.op("vector", lambda e: e.memset(ones_f16[:], 1.0), inc="dve")
            ones_sc_done = P.now("dve")
            scpart1 = asbt("scpart1", [128, NPC1])
            scpart2 = asbt("scpart2", [128, NPC2])
            spcol = asbt("spcol", [128, 2])
            dtr3 = dtr_bf[:].rearrange("p (k t) -> p k t", k=RK)
            wdt3 = wdt_bf[:].rearrange("p (k m) -> p k m", k=RK)
            P.op("gpsimd", lambda e: e.dma_start(
                out=wdt3, in_=d_wdt.rearrange("(k p) m -> p k m", p=128)),
                inc=("s_smg", 16))
            smg_all = P.now("s_smg")

            abctx = ExitStack()
            absbt = lambda name, shape, dt=F32: abctx.enter_context(
                nc.sbuf_tensor(name, shape, dt))
            wxp_bf = absbt("wxp_bf", [128, DT * (R + 2 * N)], BF16)
            srow = absbt("srow", [1, 2 * T + 64])
            xbf = [absbt(f"xbf{i}", [128, T], BF16) for i in range(DT)]
            sqt = absbt("sqt", [128, T], BF16)
            xst = [absbt(f"xst{i}", [128, T]) for i in range(4)]
            rstd_bc = absbt("rstd_bc", [128, T], BF16)
            mr_bc = absbt("mr_bc", [128, T], BF16)
            lnt = sqt
            wxp3 = wxp_bf[:].rearrange("p (k m) -> p k m", k=DT)
            P.op("gpsimd", lambda e: e.dma_start(
                out=wxp3, in_=d_wxp.rearrange("(k p) m -> p k m", p=128)),
                inc=("s_smg2", 16))
            smg2_all = P.now("s_smg2")

            x_ld = {}
            def emit_x_load(i):
                q = "sync" if i % 2 == 0 else "scalar"
                G.use(f"xst{i % 4}", q)
                x_ld[i] = P.op(q, lambda e, i=i: e.dma_start(
                    out=xst[i % 4][:], in_=d_x[i * 128:(i + 1) * 128, :]),
                    inc=(f"s_x{i % 4}", 16), dma=True)
            for i in range(min(4, DT)):
                emit_x_load(i)
            P.wait("tensor", "dve", setup_done)
            voff("p1cast")
            for i in range(DT):
                P.wait("vector", f"s_x{i % 4}", x_ld[i])
                P.op("vector", lambda e, i=i: e.tensor_copy(
                    xbf[i][:], xst[i % 4][:]), inc="dve")
                G.consumed(f"xst{i % 4}", "dve")
                if i + 4 < DT:
                    emit_x_load(i + 4)
                G.use("sqt", "vector")
                P.op("vector", lambda e, i=i: e.tensor_tensor(
                    out=sqt[:], in0=xbf[i][:], in1=xbf[i][:], op=ALU.mult),
                    inc="dve")
                sq_done = P.now("dve")
                P.wait("tensor", "dve", sq_done)
                for c in range(NC4):
                    P.op("tensor", lambda e, i=i, c=c: e.matmul(
                        bank(c, rows=1, cols=TC), ones_cb[:],
                        xbf[i][:, c * TC:(c + 1) * TC],
                        start=(i == 0), stop=(i == DT - 1)))
                for c in range(NC4):
                    P.op("tensor", lambda e, i=i, c=c: e.matmul(
                        bank(NC4 + c, rows=1, cols=TC), ones_cb[:],
                        sqt[:, c * TC:(c + 1) * TC],
                        start=(i == 0), stop=(i == DT - 1)),
                        inc="pe" if c == NC4 - 1 else None)
                G.consumed("sqt", "pe")
            stats1_done = P.now("pe")
            von()
            ckpt("p1")

            voff("ln1rows")
            sx = srow[0:1, 0:T]
            sq_r = srow[0:1, T:2 * T]
            tmp_r = srow_bf[0:1, 0:T]
            P.wait("vector", "pe", stats1_done)
            for c in range(NC4):
                P.op("vector", lambda e, c=c: e.tensor_copy(
                    sx[:, c * TC:(c + 1) * TC], bank(c, rows=1, cols=TC)))
            for c in range(NC4):
                P.op("vector", lambda e, c=c: e.tensor_copy(
                    sq_r[:, c * TC:(c + 1) * TC], bank(NC4 + c, rows=1, cols=TC)),
                    inc=("dve", 1) if c == NC4 - 1 else None)
            stats1_ev = P.now("dve")
            for b in range(8):
                G.consumed(f"bank{b}", "dve", stats1_ev)
            P.op("vector", lambda e: e.tensor_scalar(
                out=sx, in0=sx, scalar1=1.0 / D, scalar2=None, op0=ALU.mult))
            P.op("vector", lambda e: e.tensor_scalar(
                out=sq_r, in0=sq_r, scalar1=1.0 / D, scalar2=None, op0=ALU.mult))
            P.op("vector", lambda e: e.tensor_tensor(
                out=tmp_r, in0=sx, in1=sx, op=ALU.mult))
            P.op("vector", lambda e: e.tensor_tensor(
                out=sq_r, in0=sq_r, in1=tmp_r, op=ALU.subtract), inc="dve")
            var_done = P.now("dve")
            P.wait("scalar", "dve", var_done)
            P.op("scalar", lambda e: e.activation(
                out=sq_r, in_=sq_r, func=AF.Ln, bias=cvals[0:1, 0:1], scale=1.0))
            P.op("scalar", lambda e: e.activation(
                out=sq_r, in_=sq_r, func=AF.Exp, scale=-0.5), inc="act")
            rstd_done = P.now("act")
            P.wait("vector", "act", rstd_done)
            P.op("vector", lambda e: e.tensor_tensor(
                out=sx, in0=sx, in1=sq_r, op=ALU.mult))
            rrow_bf = srow_bf[0:1, 0:T]
            mrow_bf = srow_bf[0:1, T:2 * T]
            P.op("vector", lambda e: e.tensor_copy(rrow_bf, sq_r))
            P.op("vector", lambda e: e.tensor_copy(mrow_bf, sx), inc="dve")
            rows_bf_done = P.now("dve")
            P.wait("tensor", "dve", rows_bf_done)

            def k1_bcast(dst, row_ap, nch, bank_pair=(0, 1, 2, 3)):
                for c in range(nch):
                    b = bank_pair[c % len(bank_pair)]
                    G.use(f"bank{b}", "tensor")
                    v = P.op("tensor", lambda e, c=c, b=b, row_ap=row_ap:
                             e.matmul(bank(b, cols=TC), ones_rb[0:1, :],
                                      row_ap[:, c * TC:(c + 1) * TC],
                                      start=True, stop=True), inc="pe")
                    P.wait("scalar", "pe", v)
                    P.op("scalar", lambda e, c=c, b=b, dst=dst: e.activation(
                        out=dst[:, c * TC:(c + 1) * TC], in_=bank(b, cols=TC),
                        func=AF.Copy), inc="act")
                    G.consumed(f"bank{b}", "act")

            k1_bcast(rstd_bc, rrow_bf, NC4)
            k1_bcast(mr_bc, mrow_bf, NC4)
            bc1_done = P.now("act")
            von()
            ckpt("p2")

            P.wait("vector", "act", bc1_done)
            voff("p3apply")
            xap_v = {}
            for i in range(DT):
                P.op("vector", lambda e, i=i: e.tensor_tensor(
                    out=lnt[:], in0=xbf[i][:], in1=rstd_bc[:], op=ALU.mult))
                P.op("vector", lambda e, i=i: e.tensor_tensor(
                    out=xbf[i][:], in0=lnt[:], in1=mr_bc[:], op=ALU.subtract),
                    inc="dve")
                xl_v = P.now("dve")
                xap_v[i] = xl_v
                P.wait("sync", "dve", xl_v)
                P.op("sync", lambda e, i=i: e.dma_start(
                    out=d_xln[i * 128:(i + 1) * 128, :], in_=xbf[i][:]),
                    inc=("s_xlw", 16))
            xln_done = P.now("dve")
            von()
            xlw_all = P.now("s_xlw")

            P.wait("tensor", "s_sm", sm_all)
            P.wait("tensor", "s_smg", smg_all)
            P.wait("tensor", "s_smg2", smg2_all)
            mchunks = [("dtr", mi, mi * 128, 128) for mi in range(RK)] + \
                      [("B", 0, R, N), ("C", 1, R + N, N)]
            first_mchunk = [True]
            for (kind, mi, m0, msz) in mchunks:
                for c in range(NC4):
                    b = c % 2
                    G.use(f"bank{b}", "tensor")
                    v = None
                    for k in range(DT):
                        if first_mchunk[0]:
                            P.wait("tensor", "dve", xap_v[k])
                        v = P.op("tensor", lambda e, k=k, c=c, m0=m0, msz=msz, b=b:
                                 e.matmul(bank(b, rows=msz, cols=TC),
                                          wxp3[:, k, m0:m0 + msz],
                                          xbf[k][:, c * TC:(c + 1) * TC],
                                          start=(k == 0), stop=(k == DT - 1)),
                                 inc="pe" if k == DT - 1 else None)
                    if kind == "dtr":
                        P.wait("scalar", "pe", v)
                        P.op("scalar", lambda e, mi=mi, c=c, b=b: e.activation(
                            out=dtr3[:, mi, c * TC:(c + 1) * TC],
                            in_=bank(b, cols=TC), func=AF.Copy), inc="act")
                        G.consumed(f"bank{b}", "act")
                    else:
                        row0 = 0 if kind == "B" else 32
                        P.wait("vector", "pe", v)
                        P.op("vector", lambda e, row0=row0, c=c, b=b: e.tensor_copy(
                            BC_sb[row0:row0 + N, c * TC:(c + 1) * TC],
                            bank(b, rows=N, cols=TC)), inc="dve")
                        G.consumed(f"bank{b}", "dve")
                first_mchunk[0] = False
            bc_ex_done = P.now("dve")

            bcn_done = P.now("act")
            ckpt("p3")
            P.flush(nc)
            abctx.close()

            cctx = ExitStack()
            csbt = lambda name, shape, dt=F32: cctx.enter_context(
                nc.sbuf_tensor(name, shape, dt))
            B_bc = [csbt(f"Bbc{n}", [128, T], BF16) for n in range(N)]
            C_bc = [csbt(f"Cbc{n}", [128, TO], BF16) for n in range(N)]
            xln_t = [csbt(f"xlnt{i}", [128, T], BF16) for i in range(2)]
            sq2_t = csbt("sq2_t", [128, TO], BF16)
            srow2 = csbt("srow2", [1, 2 * TO + 2 * 512 + 8])
            delta_t = [csbt(f"delta_t{i}", [128, T], BF16) for i in range(2)]
            u_t = [csbt(f"u_t{i}", [128, T], BF16) for i in range(2)]
            dA_b = [csbt(f"dAb{i}", [128, T]) for i in range(2)]
            dbu_t = csbt("dbu_t", [128, T], BF16)
            h_t = csbt("h_t", [128, T], BF16)
            y_t = csbt("y_t", [128, TO], BF16)
            xo_t = csbt("xo_t", [128, TO])
            x2_t = csbt("x2_t", [128, TO])
            x2bf_t = csbt("x2bf_t", [128, TO], BF16)
            sp_t = [csbt(f"sp_t{i}", [128, TC]) for i in range(2)]
            wsc_b = [csbt(f"wscb{i}", [128, PCW]) for i in range(3)]
            ty_t = dbu_t[:, 0:TO]
            P.wait("scalar", "s_xlw", xlw_all)
            P.wait("vector", "s_xlw", xlw_all)

            P.wait("gpsimd", "dve", bc_ex_done)
            for j in range(2 * N):
                pr = j % 2
                srcrow = j if j < N else 32 + (j - N)
                G.use(f"bcr{pr}", "gpsimd")
                v = P.op("gpsimd", lambda e, pr=pr, srcrow=srcrow: e.dma_start(
                    out=bc_rows[0:1, pr * T:(pr + 1) * T],
                    in_=BC_sb[srcrow:srcrow + 1, :]), inc=(f"s_br{pr}", 16))
                P.wait("tensor", f"s_br{pr}", v)
                if j < N:
                    k1_bcast(B_bc[j], bc_rows[0:1, pr * T:(pr + 1) * T], NC4)
                else:
                    k1_bcast(C_bc[j - N],
                             bc_rows[0:1, pr * T + O0:pr * T + O0 + TO], NC2)
                G.consumed(f"bcr{pr}", "pe")
            bcn_done = P.now("act")

            sc_state = {"ld": [], "p1": 0, "p2": 0}

            def emit_scale_load(which, p):
                buf = (p if which == 1 else NPC1 + p) % 3
                src = d_w1 if which == 1 else d_w2
                G.use(f"wsc{buf}", "gpsimd")
                v = P.op("gpsimd", lambda e, p=p, buf=buf, src=src: e.dma_start(
                    out=wsc_b[buf][:], in_=src[:, p * PCW:(p + 1) * PCW]),
                    inc=(f"s_wsc{buf}", 16))
                sc_state["ld"].append((which, p, buf, v))

            def emit_scale_compute():
                if not sc_state["ld"]:
                    return False
                which, p, buf, v = sc_state["ld"].pop(0)
                P.wait("scalar", f"s_wsc{buf}", v)
                dst = scpart1 if which == 1 else scpart2
                P.op("scalar", lambda e, buf=buf, p=p, dst=dst:
                     e.activation(
                         out=wsc_b[buf][:], in_=wsc_b[buf][:], func=AF.Abs,
                         accum_out=dst[:, p:p + 1]), inc="act")
                ab_v = P.now("act")
                G.consumed(f"wsc{buf}", "act", ab_v)
                if which == 1:
                    sc_state["p1"] = p + 1
                    if p + 1 == NPC1:
                        sc_state["w1_act"] = ab_v
                else:
                    sc_state["p2"] = p + 1
                    if p + 1 == NPC2:
                        sc_state["w2_act"] = ab_v
                return True

            sc_next = [0, 0]
            def queue_next_load():
                if sc_next[0] < NPC1:
                    emit_scale_load(1, sc_next[0])
                    sc_next[0] += 1
                elif sc_next[1] < NPC2:
                    emit_scale_load(2, sc_next[1])
                    sc_next[1] += 1
            for _ in range(3):
                queue_next_load()

            def pump_scale(nsteps):
                for _ in range(nsteps):
                    if not emit_scale_compute():
                        return
                    queue_next_load()

            tsc = srow2[0:1, 2 * TO + 2 * 512:2 * TO + 2 * 512 + 2]
            mrow = srow2[0:1, 2 * TO + 2 * 512 + 2:2 * TO + 2 * 512 + 8]
            thr_ready = {}

            def emit_thr(j):
                P.vforce = True
                part = scpart1 if j == 0 else scpart2
                acc_done = sc_state["w1_act"] if j == 0 else sc_state["w2_act"]
                sl = srow2[0:1, 2 * TO + j * 128:2 * TO + (j + 1) * 128]
                sl2 = srow2[0:1, 2 * TO + 256 + j * 128:2 * TO + 256 + (j + 1) * 128]
                P.wait("vector", "act", acc_done)
                P.op("vector", lambda e: e.tensor_reduce(
                    out=spcol[:, j:j + 1], in_=part[:], axis=mybir.AxisListType.X,
                    op=ALU.add), inc="dve")
                spc_v = P.now("dve")
                P.wait("sync", "dve", spc_v)
                P.op("sync", lambda e: e.dma_start(
                    out=d_scp[:, j:j + 1], in_=spcol[:, j:j + 1],
                    allow_slow_non_contiguous=True), inc=("s_thr", 16))
                P.wait("sync", "s_thr", P.now("s_thr"))
                P.op("sync", lambda e: e.dma_start(
                    out=sl,
                    in_=d_scp[:, j:j + 1].rearrange("p n -> (p n)")[None, :],
                    allow_slow_non_contiguous=True), inc=("s_thr", 16))
                P.wait("vector", "s_thr", P.now("s_thr"))
                P.op("vector", lambda e: e.tensor_reduce(
                    out=mrow[:, j:j + 1], in_=sl, axis=mybir.AxisListType.X,
                    op=ALU.add))
                P.op("vector", lambda e: e.tensor_scalar(
                    out=mrow[:, j:j + 1], in0=mrow[:, j:j + 1],
                    scalar1=1.0 / 128, scalar2=None, op0=ALU.mult))
                P.op("vector", lambda e: e.tensor_scalar(
                    out=sl2, in0=sl, scalar1=mrow[:, j:j + 1], scalar2=None,
                    op0=ALU.subtract))
                P.op("vector", lambda e: e.tensor_reduce(
                    out=mrow[:, 2 + j:3 + j], in_=sl2, axis=mybir.AxisListType.X,
                    op=ALU.add))
                P.op("vector", lambda e: e.tensor_scalar(
                    out=tsc[:, j:j + 1], in0=mrow[:, j:j + 1], scalar1=128.0,
                    scalar2=mrow[:, 2 + j:3 + j], op0=ALU.mult, op1=ALU.add))
                P.op("vector", lambda e: e.tensor_scalar(
                    out=tsc[:, j:j + 1], in0=tsc[:, j:j + 1], scalar1=1.0 / NW,
                    scalar2=cfg["W_EPS"], op0=ALU.mult, op1=ALU.max))
                P.op("vector", lambda e: e.tensor_scalar(
                    out=tsc[:, j:j + 1], in0=tsc[:, j:j + 1], scalar1=0.5,
                    scalar2=None, op0=ALU.mult), inc="dve")
                thr_v = P.now("dve")
                P.wait("sync", "dve", thr_v)
                P.op("sync", lambda e: e.dma_start(
                    out=d_thr[0:1, j:j + 1], in_=tsc[:, j:j + 1],
                    allow_slow_non_contiguous=True), inc=("s_thr", 16))
                P.wait("sync", "s_thr", P.now("s_thr"))
                P.op("sync", lambda e: e.dma_start(
                    out=thr_cols[:, 2 * j:2 * j + 1],
                    in_=d_thr[0:1, j:j + 1].partition_broadcast(128)),
                    inc=("s_thr", 16))
                tb = P.now("s_thr")
                P.wait("vector", "s_thr", tb)
                P.op("vector", lambda e: e.tensor_scalar(
                    out=thr_cols[:, 2 * j + 1:2 * j + 2],
                    in0=thr_cols[:, 2 * j:2 * j + 1], scalar1=-1.0,
                    scalar2=None, op0=ALU.mult), inc="dve")
                thr_ready[j] = P.now("dve")
                P.vforce = False

            pump_per_dt = (NPC1 + NPC2 + DT - 1) // DT

            def pump_point(state):
                if state["budget"] > 0:
                    pump_scale(1)
                    state["budget"] -= 1
                if "w1_act" in sc_state and 0 not in thr_ready:
                    emit_thr(0)
            P.wait("sync", "s_xlw", xlw_all)
            xl_rd = {}
            def emit_xln_read(dt):
                G.use(f"xlt{dt % 2}", "sync")
                xl_rd[dt] = P.op("sync", lambda e, dt=dt: e.dma_start(
                    out=xln_t[dt % 2][:], in_=d_xln[dt * 128:(dt + 1) * 128, :]),
                    inc=(f"s_xl{dt % 2}", 16))
            for dt in range(min(2, DT)):
                emit_xln_read(dt)

            delta_done = {}
            def emit_delta_chunk(dt, c):
                dp = dt % 2
                b = c % 2
                G.use(f"bank{b}", "tensor")
                v = None
                for k in range(RK):
                    v = P.op("tensor", lambda e, k=k, c=c, dt=dt, b=b:
                             e.matmul(bank(b, cols=TC),
                                      wdt3[:, k, dt * 128:(dt + 1) * 128],
                                      dtr3[:, k, c * TC:(c + 1) * TC],
                                      start=(k == 0), stop=(k == RK - 1)),
                             inc="pe" if k == RK - 1 else None)
                P.wait("scalar", "pe", v)
                if c == 0:
                    G.use(f"delta{dp}", "scalar")
                P.op("scalar", lambda e, c=c, dt=dt, b=b, dp=dp: e.activation(
                    out=delta_t[dp][:, c * TC:(c + 1) * TC],
                    in_=bank(b, cols=TC),
                    func=AF.Relu, bias=bdt_c[:, dt:dt + 1], scale=1.0))
                G.use(f"sp{c % 2}", "scalar")
                P.op("scalar", lambda e, c=c, dt=dt, b=b: e.activation(
                    out=sp_t[c % 2][:], in_=bank(b, cols=TC), func=AF.Abs,
                    bias=bdt_c[:, dt:dt + 1], scale=1.0), inc="act")
                G.consumed(f"bank{b}", "act")
                P.op("scalar", lambda e, c=c: e.activation(
                    out=sp_t[c % 2][:], in_=sp_t[c % 2][:], func=AF.Exp,
                    scale=-1.0))
                P.op("scalar", lambda e, c=c: e.activation(
                    out=sp_t[c % 2][:], in_=sp_t[c % 2][:], func=AF.Ln,
                    bias=cvals[:, 1:2], scale=1.0), inc="act")
                spl_v = P.now("act")
                P.wait("vector", "act", spl_v)
                P.op("vector", lambda e, c=c, dp=dp: e.tensor_tensor(
                    out=delta_t[dp][:, c * TC:(c + 1) * TC],
                    in0=delta_t[dp][:, c * TC:(c + 1) * TC], in1=sp_t[c % 2][:],
                    op=ALU.add), inc="dve")
                G.consumed(f"sp{c % 2}", "dve")
                if c == NC4 - 1:
                    delta_done[dt] = P.now("dve")

            def emit_u(dt):
                dp = dt % 2
                P.wait("vector", f"s_xl{dp}", xl_rd[dt])
                G.use(f"u{dp}", "vector")
                P.op("vector", lambda e, dp=dp: e.tensor_tensor(
                    out=u_t[dp][:], in0=delta_t[dp][:], in1=xln_t[dp][:],
                    op=ALU.mult), inc="dve")

            voff("p4")
            for c in range(NC4):
                emit_delta_chunk(0, c)
            emit_u(0)
            for dt in range(DT):
                dp = dt % 2
                pstate = {"budget": pump_per_dt}
                G.use("xo_t", "sync")
                xo_v = P.op("sync", lambda e, dt=dt: e.dma_start(
                    out=xo_t[:], in_=d_x[dt * 128:(dt + 1) * 128, O0:T]),
                    inc=("s_xo", 16))
                if dt == 0:
                    ckpt("p4a")
                for n in range(N):
                    nb = n % 2
                    G.use(f"dA{nb}", "scalar")
                    P.wait("scalar", "dve", delta_done[dt])
                    av = P.op("scalar", lambda e, n=n, dt=dt, nb=nb, dp=dp:
                              e.activation(
                                  out=dA_b[nb][:], in_=delta_t[dp][:],
                                  func=AF.Exp,
                                  scale=negA[:, dt * N + n:dt * N + n + 1]),
                              inc="act")
                    P.op("vector", lambda e, n=n, dp=dp: e.tensor_tensor(
                        out=dbu_t[:], in0=u_t[dp][:], in1=B_bc[n][:],
                        op=ALU.mult))
                    P.wait("vector", "act", av)
                    P.vforce = True
                    P.op("vector", lambda e, nb=nb: e.tensor_tensor_scan(
                        out=h_t[:], data0=dA_b[nb][:], data1=dbu_t[:],
                        initial=0.0, op0=ALU.mult, op1=ALU.add), inc="dve")
                    P.vforce = False
                    G.consumed(f"dA{nb}", "dve")
                    if n == 0:
                        P.op("vector", lambda e, n=n: e.tensor_tensor(
                            out=y_t[:], in0=h_t[:, O0:T], in1=C_bc[n][:],
                            op=ALU.mult))
                    else:
                        P.op("vector", lambda e, n=n: e.tensor_tensor(
                            out=ty_t, in0=h_t[:, O0:T], in1=C_bc[n][:],
                            op=ALU.mult))
                        P.op("vector", lambda e: e.tensor_tensor(
                            out=y_t[:], in0=y_t[:], in1=ty_t, op=ALU.add))
                    if dt + 1 < DT and n < NC4:
                        emit_delta_chunk(dt + 1, n)
                    if dt + 1 < DT and n == NC4:
                        emit_u(dt + 1)
                    if n in (0, 2, 4, 6):
                        pump_point(pstate)
                G.consumed(f"delta{dp}", "act")
                G.consumed(f"u{dp}", "dve")
                if dt == 0:
                    ckpt("p4b")
                P.op("vector", lambda e, dt=dt, dp=dp: e.scalar_tensor_tensor(
                    out=y_t[:], in0=xln_t[dp][:, O0:T],
                    scalar=dpar_c[:, dt:dt + 1], in1=y_t[:],
                    op0=ALU.mult, op1=ALU.add))
                G.consumed(f"xlt{dp}", "dve")
                if dt + 2 < DT:
                    emit_xln_read(dt + 2)
                P.wait("vector", "s_xo", xo_v)
                G.use("x2_t", "vector")
                P.op("vector", lambda e: e.tensor_tensor(
                    out=x2_t[:], in0=xo_t[:], in1=y_t[:], op=ALU.add),
                    inc="dve")
                x2_v = P.now("dve")
                G.consumed("xo_t", "dve")
                P.wait("sync", "dve", x2_v)
                P.op("sync", lambda e, dt=dt: e.dma_start(
                    out=d_x2[dt * 128:(dt + 1) * 128, :], in_=x2_t[:]),
                    inc=("s_x2w", 16))
                G.consumed("x2_t", "s_x2w")
                G.use("x2bf_t", "vector")
                P.op("vector", lambda e: e.tensor_copy(x2bf_t[:], x2_t[:]))
                G.use("sq2", "vector")
                P.op("vector", lambda e: e.tensor_tensor(
                    out=sq2_t[:], in0=x2bf_t[:], in1=x2bf_t[:], op=ALU.mult),
                    inc="dve")
                s2_done = P.now("dve")
                P.wait("tensor", "dve", s2_done)
                if dt == 0:
                    for b in range(4, 8):
                        G.use(f"bank{b}", "tensor")
                for c in range(NC2):
                    P.op("tensor", lambda e, c=c, dt=dt: e.matmul(
                        bank(4 + c, rows=1, cols=TC), ones_cb[:],
                        x2bf_t[:, c * TC:(c + 1) * TC],
                        start=(dt == 0), stop=(dt == DT - 1)))
                for c in range(NC2):
                    P.op("tensor", lambda e, c=c, dt=dt: e.matmul(
                        bank(4 + NC2 + c, rows=1, cols=TC), ones_cb[:],
                        sq2_t[:, c * TC:(c + 1) * TC],
                        start=(dt == 0), stop=(dt == DT - 1)),
                        inc="pe" if c == NC2 - 1 else None)
                G.consumed("sq2", "pe")
                G.consumed("x2bf_t", "pe")
                if dt == 0:
                    ckpt("p4c")
                while pstate["budget"] > 0:
                    pump_point(pstate)
                if dt == 0:
                    ckpt("p4d")
            stats2_done = P.now("pe")
            von()
            ckpt("p4")
            pump_scale(NPC1 + NPC2)
            if 0 not in thr_ready:
                emit_thr(0)
            emit_thr(1)

            ckpt("thr", dump=(thr_cols[:], 128, 4))

            voff("ln2rows")
            s2x = srow2[0:1, 0:TO]
            s2q = srow2[0:1, TO:2 * TO]
            tmp2 = srow_bf[0:1, TO:2 * TO]
            P.wait("vector", "pe", stats2_done)
            for c in range(NC2):
                P.op("vector", lambda e, c=c: e.tensor_copy(
                    s2x[:, c * TC:(c + 1) * TC], bank(4 + c, rows=1, cols=TC)))
            for c in range(NC2):
                P.op("vector", lambda e, c=c: e.tensor_copy(
                    s2q[:, c * TC:(c + 1) * TC],
                    bank(4 + NC2 + c, rows=1, cols=TC)),
                    inc=("dve", 1) if c == NC2 - 1 else None)
            stats2_ev = P.now("dve")
            for b in range(4, 8):
                G.consumed(f"bank{b}", "dve", stats2_ev)
            P.op("vector", lambda e: e.tensor_scalar(
                out=s2x, in0=s2x, scalar1=1.0 / D, scalar2=None, op0=ALU.mult))
            P.op("vector", lambda e: e.tensor_scalar(
                out=s2q, in0=s2q, scalar1=1.0 / D, scalar2=None, op0=ALU.mult))
            P.op("vector", lambda e: e.tensor_tensor(
                out=tmp2, in0=s2x, in1=s2x, op=ALU.mult))
            P.op("vector", lambda e: e.tensor_tensor(
                out=s2q, in0=s2q, in1=tmp2, op=ALU.subtract), inc="dve")
            var2_done = P.now("dve")
            P.wait("scalar", "dve", var2_done)
            P.op("scalar", lambda e: e.activation(
                out=s2q, in_=s2q, func=AF.Ln, bias=cvals[0:1, 0:1], scale=1.0))
            P.op("scalar", lambda e: e.activation(
                out=s2q, in_=s2q, func=AF.Exp, scale=-0.5), inc="act")
            r2_done = P.now("act")
            P.wait("vector", "act", r2_done)
            P.op("vector", lambda e: e.tensor_tensor(
                out=s2x, in0=s2x, in1=s2q, op=ALU.mult))
            P.op("vector", lambda e: e.tensor_copy(srow_bf[0:1, 0:TO], s2q))
            P.op("vector", lambda e: e.tensor_copy(srow_bf[0:1, T:T + TO], s2x),
                 inc="dve")
            rows2_bf = P.now("dve")
            von()
            P.flush(nc)
            cctx.close()

        bctx = ExitStack()
        with bctx:
            def bsbt(name, shape, dt=F32):
                return bctx.enter_context(nc.sbuf_tensor(name, shape, dt))
            h_sb = bsbt("h_sb", [128, FT * TO], BF16)
            f1ctx = ExitStack()
            f1sbt = lambda name, shape, dt=F32: f1ctx.enter_context(
                nc.sbuf_tensor(name, shape, dt))
            xln2 = [f1sbt(f"xln2_{i}", [128, TO], BF16) for i in range(DT)]
            r2_bc = f1sbt("r2_bc", [128, TO], BF16)
            m2_bc = f1sbt("m2_bc", [128, TO], BF16)
            x2r_t = [f1sbt(f"x2rt{i}", [128, TO]) for i in range(2)]
            w1s_b = [f1sbt(f"w1sb{i}", [128, DT * 128]) for i in range(2)]
            w1q_b = [f1sbt(f"w1qb{i}", [128, DT * 128], BF16) for i in range(2)]

            w1u_v = {}
            SLW1 = DT * 128
            def emit_w1_load(ft):
                G.use(f"w1s{ft % 2}", "sync")
                w1u_v[ft] = P.op("sync", lambda e, ft=ft: e.dma_start(
                    out=w1s_b[ft % 2][:],
                    in_=d_w1[:, ft * SLW1:(ft + 1) * SLW1]),
                    inc=(f"s_w1{ft % 2}", 16))
            tern1_v = {}
            def emit_w1_tern(ft):
                voff("ffn1")
                fb = ft % 2
                P.wait("vector", f"s_w1{fb}", w1u_v[ft])
                G.use(f"w1q{fb}", "vector")
                P.op("vector", lambda e, fb=fb: e.tensor_scalar(
                    out=w1q_b[fb][:], in0=w1s_b[fb][:], scalar1=thr_cols[:, 1:2],
                    scalar2=None, op0=ALU.is_lt))
                P.op("vector", lambda e, fb=fb: e.scalar_tensor_tensor(
                    out=w1q_b[fb][:], in0=w1s_b[fb][:], scalar=thr_cols[:, 0:1],
                    in1=w1q_b[fb][:], op0=ALU.is_gt, op1=ALU.subtract), inc="dve")
                tern1_v[ft] = P.now("dve")
                von()
                G.consumed(f"w1s{fb}", "dve")
            for ft in range(min(2, FT)):
                emit_w1_load(ft)
            P.wait("vector", "dve", thr_ready[0])
            for ft in range(min(2, FT)):
                emit_w1_tern(ft)

            P.wait("tensor", "dve", rows2_bf)
            k1_bcast(r2_bc, srow_bf[0:1, 0:TO], NC2, bank_pair=(0, 1))
            k1_bcast(m2_bc, srow_bf[0:1, T:T + TO], NC2, bank_pair=(0, 1))
            bc2_done = P.now("act")

            P.wait("vector", "act", bc2_done)
            voff("xln2")
            x2r_v = {}
            def emit_x2r(i):
                G.use(f"x2r{i % 2}", "sync")
                P.wait("sync", "s_x2w", 16 * (i + 1))
                x2r_v[i] = P.op("sync", lambda e, i=i: e.dma_start(
                    out=x2r_t[i % 2][:], in_=d_x2[i * 128:(i + 1) * 128, :]),
                    inc=(f"s_x2r{i % 2}", 16))
            for i in range(min(2, DT)):
                emit_x2r(i)
            xln2_v = {}
            for i in range(DT):
                P.wait("vector", f"s_x2r{i % 2}", x2r_v[i])
                P.op("vector", lambda e, i=i: e.tensor_tensor(
                    out=xln2[i][:], in0=x2r_t[i % 2][:], in1=r2_bc[:],
                    op=ALU.mult))
                P.op("vector", lambda e, i=i: e.tensor_tensor(
                    out=xln2[i][:], in0=xln2[i][:], in1=m2_bc[:], op=ALU.subtract),
                    inc="dve")
                xln2_v[i] = P.now("dve")
                G.consumed(f"x2r{i % 2}", "dve")
                if i + 2 < DT:
                    emit_x2r(i + 2)
            xln2_done = P.now("dve")
            von()
            ckpt("xln2")

            for ft in range(FT):
                fb = ft % 2
                P.wait("tensor", "dve", tern1_v[ft])
                if ft == 0:
                    P.wait("tensor", "act", bc2_done)
                for c in range(NC2):
                    b = (ft * NC2 + c) % 2
                    G.use(f"bank{b}", "tensor")
                    v = None
                    for k in range(DT):
                        if ft == 0:
                            P.wait("tensor", "dve", xln2_v[k])
                        v = P.op("tensor", lambda e, k=k, c=c, fb=fb, b=b:
                                 e.matmul(bank(b, cols=TC),
                                          w1q_b[fb][:, k * 128:(k + 1) * 128],
                                          xln2[k][:, c * TC:(c + 1) * TC],
                                          start=(k == 0), stop=(k == DT - 1)),
                                 inc="pe" if k == DT - 1 else None)
                    P.wait("scalar", "pe", v)
                    P.op("scalar", lambda e, ft=ft, c=c, b=b: e.activation(
                        out=h_sb[:, ft * TO + c * TC: ft * TO + (c + 1) * TC],
                        in_=bank(b, cols=TC), func=AF.Gelu,
                        bias=b1_c[:, ft:ft + 1], scale=1.0), inc="act")
                    G.consumed(f"bank{b}", "act")
                G.consumed(f"w1q{fb}", "pe")
                if ft + 2 < FT:
                    emit_w1_load(ft + 2)
                    emit_w1_tern(ft + 2)
            ffn1_done = P.now("act")
            ckpt("ffn1")
            P.flush(nc)
            f1ctx.close()

            f2ctx = bctx.enter_context(ExitStack())
            f2sbt = lambda name, shape, dt=F32: f2ctx.enter_context(
                nc.sbuf_tensor(name, shape, dt))
            w2s_b = [f2sbt(f"w2sb{i}", [128, KSK * DQW]) for i in range(4)]
            w2q_b = [f2sbt(f"w2qb{i}", [128, KSK * DQW], BF16) for i in range(3)]
            oev_b = [f2sbt(f"oevb{i}", [128, TC]) for i in range(2)]
            x2e_b = [f2sbt(f"x2eb{i}", [128, TC]) for i in range(4)]
            SLW2 = KSK * DQW
            w2u_v = {}
            def emit_w2_load(idx):
                q = "sync" if idx % 2 == 0 else "scalar"
                G.use(f"w2s{idx % 4}", q)
                w2u_v[idx] = P.op(q, lambda e, idx=idx: e.dma_start(
                    out=w2s_b[idx % 4][:],
                    in_=d_w2[:, idx * SLW2:(idx + 1) * SLW2]),
                    inc=(f"s_w2s{idx % 4}", 16), dma=True)
            for idx in range(min(4, DQ * NKS)):
                emit_w2_load(idx)
            P.wait("tensor", "act", ffn1_done)
            P.wait("vector", "dve", thr_ready[1])
            voff("ffn2")

            x2e_v = {}
            def emit_x2e_loads(dq):
                for j in range(DQT * NC2):
                    ck = dq * DQT * NC2 + j
                    gdt = dq * DQT + j // NC2
                    c = j % NC2
                    eb = ck % 4
                    G.use(f"x2eb{eb}", "sync")
                    x2e_v[ck] = P.op("sync", lambda e, gdt=gdt, c=c, eb=eb:
                                     e.dma_start(
                                         out=x2e_b[eb][:],
                                         in_=d_x2[gdt * 128:(gdt + 1) * 128,
                                                  c * TC:(c + 1) * TC]),
                                     inc=(f"s_x2e{eb}", 16))
            evst = {"chunk": 0}
            def emit_evacs(dq, mmv):
                base = (dq % 2) * 4 if DQ > 1 else 0
                for dtq in range(DQT):
                    gdt = dq * DQT + dtq
                    for c in range(NC2):
                        ck = evst["chunk"]
                        eb = ck % 4
                        evst["chunk"] += 1
                        ob = ck % 2
                        P.wait("vector", "pe", mmv)
                        P.wait("vector", f"s_x2e{eb}", x2e_v[ck])
                        G.use(f"oev{ob}", "vector")
                        bnum = base + dtq * NC2 + c
                        P.op("vector", lambda e, gdt=gdt, bnum=bnum, eb=eb,
                             ob=ob:
                             e.scalar_tensor_tensor(
                                 out=oev_b[ob][:],
                                 in0=ps[0:128, bnum * 512:bnum * 512 + TC],
                                 scalar=b2_c[:, gdt:gdt + 1], in1=x2e_b[eb][:],
                                 op0=ALU.add, op1=ALU.add), inc="dve")
                        ev = P.now("dve")
                        G.consumed(f"x2eb{eb}", "dve", ev)
                        G.consumed(f"bank{bnum}", "dve", ev)
                        P.wait("sync", "dve", ev)
                        P.op("sync", lambda e, gdt=gdt, c=c, ob=ob: e.dma_start(
                            out=d_out[gdt * 128:(gdt + 1) * 128,
                                      c * TC:(c + 1) * TC],
                            in_=oev_b[ob][:]), inc=(f"s_out{ob}", 16))
                        G.consumed(f"oev{ob}", f"s_out{ob}")

            emit_x2e_loads(0)
            pending = None
            for dq in range(DQ):
                base = (dq % 2) * 4 if DQ > 1 else 0
                for b in range(base, base + DQT * NC2):
                    G.use(f"bank{b}", "tensor")
                for ks in range(NKS):
                    idx = dq * NKS + ks
                    sb4 = idx % 4
                    sb2 = idx % 3
                    P.wait("vector", f"s_w2s{sb4}", w2u_v[idx])
                    G.use(f"w2q{sb2}", "vector")
                    P.op("vector", lambda e, sb2=sb2, sb4=sb4: e.tensor_scalar(
                        out=w2q_b[sb2][:], in0=w2s_b[sb4][:],
                        scalar1=thr_cols[:, 3:4],
                        scalar2=None, op0=ALU.is_lt))
                    P.op("vector", lambda e, sb2=sb2, sb4=sb4:
                         e.scalar_tensor_tensor(
                             out=w2q_b[sb2][:], in0=w2s_b[sb4][:],
                             scalar=thr_cols[:, 2:3], in1=w2q_b[sb2][:],
                             op0=ALU.is_gt, op1=ALU.subtract), inc="dve")
                    tern2_v = P.now("dve")
                    G.consumed(f"w2s{sb4}", "dve")
                    if ks == min(1, NKS - 1) and pending is not None:
                        emit_evacs(*pending)
                        pending = None
                        emit_x2e_loads(dq)
                    P.wait("tensor", "dve", tern2_v)
                    for kk in range(KSK):
                        kglob = ks * KSK + kk
                        for dtq in range(DQT):
                            for c in range(NC2):
                                bnum = base + dtq * NC2 + c
                                lastk = (kk == KSK - 1 and dtq == DQT - 1
                                         and c == NC2 - 1)
                                P.op("tensor", lambda e, kk=kk, dtq=dtq, c=c,
                                     sb2=sb2, kglob=kglob, bnum=bnum, ks=ks:
                                     e.matmul(
                                         ps[0:128, bnum * 512:bnum * 512 + TC],
                                         w2q_b[sb2][:, kk * DQW + dtq * 128:
                                                    kk * DQW + (dtq + 1) * 128],
                                         h_sb[:, kglob * TO + c * TC:
                                              kglob * TO + (c + 1) * TC],
                                         start=(ks == 0 and kk == 0),
                                         stop=(ks == NKS - 1 and kk == KSK - 1)),
                                     inc="pe" if lastk else None)
                    G.consumed(f"w2q{sb2}", "pe")
                    if idx + 4 < DQ * NKS:
                        emit_w2_load(idx + 4)
                mm_done = P.now("pe")
                pending = (dq, mm_done)
            emit_evacs(*pending)
            von()
            P.wait("gpsimd", "s_out0", P.now("s_out0"))
            P.wait("gpsimd", "s_out1", P.now("s_out1"))
            P.flush(nc)
    except _StopBuild:
        pass
    return nc


_CACHE = {}

def _build_full():
    if "nc" not in _CACHE:
        _CACHE["nc"] = build_nc(make_cfg(full=True))
    return _CACHE["nc"]


def kernel(L0_attn_x, L0_attn_W_xproj, L0_attn_W_dt, L0_attn_b_dt, L0_attn_A_log,
           L0_attn_D_param, L0_ffn_w1, L0_ffn_b1, L0_ffn_w2, L0_ffn_b2):
    x = np.asarray(L0_attn_x, np.float32)
    Wxp = np.asarray(L0_attn_W_xproj, np.float32)
    Wdt = np.asarray(L0_attn_W_dt, np.float32)
    bdt = np.asarray(L0_attn_b_dt, np.float32)
    Alog = np.asarray(L0_attn_A_log, np.float32)
    Dpar = np.asarray(L0_attn_D_param, np.float32)
    w1 = np.asarray(L0_ffn_w1, np.float32)
    b1 = np.asarray(L0_ffn_b1, np.float32)
    w2 = np.asarray(L0_ffn_w2, np.float32)
    b2 = np.asarray(L0_ffn_b2, np.float32)

    B, L, D = x.shape
    TO = L // 2
    N = Alog.shape[1]
    DT = D // 128
    FT = w1.shape[0] // 128

    wxpT = np.ascontiguousarray(Wxp.T)
    wdtT = np.ascontiguousarray(Wdt.T)
    Fdim = w1.shape[0]
    DQW = min(256, D)
    w1T = np.ascontiguousarray(
        w1.T.reshape(DT, 128, FT, 128).transpose(1, 2, 0, 3).reshape(128, -1))
    w2T = np.ascontiguousarray(
        w2.T.reshape(Fdim // 128, 128, D // DQW, DQW)
        .transpose(1, 2, 0, 3).reshape(128, -1))
    bdt_l = np.ascontiguousarray(bdt.reshape(DT, 128).T)
    dpar_l = np.ascontiguousarray(Dpar.reshape(DT, 128).T)
    alog_l = np.ascontiguousarray(
        Alog.reshape(DT, 128, N).transpose(1, 0, 2).reshape(128, DT * N))
    b1_l = np.ascontiguousarray(b1.reshape(FT, 128).T)
    b2_l = np.ascontiguousarray(b2.reshape(DT, 128).T)

    in_maps = []
    for c in range(8):
        b, r = c // 2, c % 2
        xT = np.ascontiguousarray(x[b].T)
        own = xT[:, r * TO:(r + 1) * TO]
        prefix = np.zeros((D, TO), np.float32) if r == 0 else xT[:, 0:TO]
        x_c = np.ascontiguousarray(np.concatenate([prefix, own], axis=1))
        in_maps.append({
            "x": x_c, "wxp": wxpT, "wdt": wdtT, "bdt": bdt_l, "alog": alog_l,
            "dpar": dpar_l, "w1t": w1T, "b1": b1_l, "w2t": w2T, "b2": b2_l,
        })

    nc = _build_full()
    res = run_bass_kernel_spmd(nc, in_maps, core_ids=list(range(8)))

    out = np.empty((B, L, D), np.float32)
    for c in range(8):
        b, r = c // 2, c % 2
        out[b, r * TO:(r + 1) * TO, :] = res.results[c]["out"].T
    return out
